# version 01
# baseline (speedup 1.0000x reference)
import numpy as np
from contextlib import ExitStack
import concourse.bass as bass
import concourse.mybir as mybir
from concourse.bass_utils import run_bass_kernel_spmd
from concourse.alu_op_type import AluOpType as OP

F32 = mybir.dt.float32
AF = mybir.ActivationFunctionType

B, T, D = 16, 2048, 1024
NCORES = 8
SPC = 2
NCH = T // 128
HID = 4096
QW = 122
NEG = -1.0e30
ENGS = ("tensor", "vector", "scalar", "sync", "gpsimd")
AX = mybir.AxisListType.X


class Emit:
    CHAINED = ("vector", "scalar")

    def __init__(self, nc):
        self.nc = nc
        self.block = None
        self.buf = {e: [] for e in ENGS}
        self.sems = {}
        self.cnt = {}
        self.nbar = 0
        self.last_evt = {e: None for e in ENGS}

    def sem(self, name):
        if name not in self.sems:
            self.sems[name] = self.nc.alloc_semaphore(name)
            self.cnt[name] = 0
        return self.sems[name]

    def val(self, name):
        self.sem(name)
        return self.cnt[name]

    def op(self, eng, build, waits=(), inc=None, inc_amt=1):
        waits = list(waits)
        if eng in self.CHAINED:
            if self.last_evt[eng] is not None:
                waits.append(self.last_evt[eng])
            if inc is None:
                inc = eng[0] + "self"
                inc_amt = 1
        waits = tuple((s, v) for (s, v) in waits if v > 0)
        for (s, v) in waits:
            assert s in self.cnt and v <= self.cnt[s], (s, v, self.cnt.get(s))
        h = None
        if inc is not None:
            h = self.sem(inc)
            self.cnt[inc] += inc_amt
            if eng in self.CHAINED:
                self.last_evt[eng] = (inc, self.cnt[inc])

        def f(en, build=build, waits=waits, h=h, amt=inc_amt):
            for (s, v) in waits:
                en.wait_ge(self.sems[s], v)
            ins = build(en)
            if h is not None:
                ins.then_inc(h, amt)
        self.buf[eng].append(f)

    def dma(self, eng, out, in_, waits=(), inc=None, **kw):
        if inc is None and eng in self.CHAINED:
            inc = eng[0] + "dma"
        self.op(eng, lambda en: en.dma_start(out=out, in_=in_, **kw),
                waits=waits, inc=inc, inc_amt=16)

    def flush(self):
        for e in ENGS:
            ops = self.buf[e]
            self.buf[e] = []
            if ops:
                getattr(self.block, e)(lambda en, ops=ops: [f(en) for f in ops])

    def barrier(self):
        self.nbar += 1
        tgt = self.nbar * len(ENGS)
        bar = self.sem("bar")
        for e in ENGS:
            w = self.last_evt[e]
            def f(en, tgt=tgt, w=w):
                if w is not None and w[1] > 0:
                    en.wait_ge(self.sems[w[0]], w[1])
                en.nop().then_inc(bar, 1)
                en.wait_ge(bar, tgt)
            self.buf[e].append(f)
        self.cnt["bar"] = tgt
        self.flush()



def build_attn_a_consts():
    KS = 4
    NW_T = (T - KS) // 2 + 1
    wins = [(2 * n, 2 * n + KS) for n in range(NW_T)]
    subs = []
    j = 0
    while j * QW < T:
        tq = j * QW
        w = min(QW, T - tq)
        s0 = max(tq - 3, 0)
        s1 = min(tq + w + 3, T)
        n0 = max(0, -(-(tq - 3) // 2))
        n1 = min(NW_T - 1, (tq + w - 1) // 2)
        subs.append((tq, w, s0, s1 - s0, n0, n1 - n0 + 1))
        j += 1

    def tables(tq, w, s0, sw, n0, nw):
        mask = np.full((128, QW), NEG, np.float32)
        sel = np.zeros((128, 64), np.float32)
        selT = np.zeros((64, 128), np.float32)
        m2 = np.zeros((64, QW), np.float32)
        for nl in range(nw):
            lo, hi = wins[n0 + nl]
            for s in range(lo, hi):
                sl = s - s0
                if 0 <= sl < sw:
                    sel[sl, nl] = 1.0
                    selT[nl, sl] = 1.0
            for t in range(lo, hi):
                tl = t - tq
                if 0 <= tl < w:
                    m2[nl, tl] = 1.0
                    for s in range(lo, hi):
                        sl = s - s0
                        if 0 <= sl < sw:
                            mask[sl, tl] = 0.0
        return mask, sel, selT, m2

    var_of, packs = [], {}
    edge = {}
    cnt = np.zeros(T)
    for (lo, hi) in wins:
        cnt[lo:hi] += 1
    for (i, st) in enumerate(subs):
        v = 0 if i == 0 else (2 if i == len(subs) - 1 else 1)
        var_of.append(v)
        tb = tables(*st)
        tq, w = st[0], st[1]
        er = np.zeros((1, 128), np.float32)
        for tl in range(w):
            if cnt[tq + tl] == 1:
                er[0, tl] = -1.0
        if v not in packs:
            packs[v] = tb
            edge[v] = er
        else:
            for a, b in zip(packs[v], tb):
                assert np.array_equal(a, b)
            assert np.array_equal(edge[v], er)
    return dict(CM=np.stack([packs[v][0] for v in range(3)]),
                CS=np.stack([packs[v][1] for v in range(3)]),
                CST=np.stack([packs[v][2] for v in range(3)]),
                CM2=np.stack([packs[v][3] for v in range(3)]),
                CE=np.stack([edge[v] for v in range(3)]),
                subs=subs, var_of=var_of)


ACONSTS = build_attn_a_consts()


def host_consts():
    return {
        "C_ident": np.eye(128, dtype=np.float32),
        "C_triu": np.triu(np.ones((128, 128), np.float32)),
        "CA_mask": ACONSTS["CM"], "CA_sel": ACONSTS["CS"],
        "CA_selT": ACONSTS["CST"], "CA_m2": ACONSTS["CM2"],
        "CA_edge": ACONSTS["CE"],
    }


def emit_stats(E, xap, SQ, AG):
    E.op("vector", lambda en: en.tensor_tensor(out=SQ, in0=xap, in1=xap, op=OP.mult))
    E.op("vector", lambda en: en.reduce_sum(AG[:, 0:1], xap, axis=AX))
    E.op("vector", lambda en: en.reduce_sum(AG[:, 1:2], SQ, axis=AX))
    E.op("vector", lambda en: en.tensor_scalar(
        out=AG[:, 0:1], in0=AG[:, 0:1], scalar1=1.0 / 1024.0, scalar2=None, op0=OP.mult))
    E.op("vector", lambda en: en.tensor_tensor(
        out=AG[:, 3:4], in0=AG[:, 0:1], in1=AG[:, 0:1], op=OP.mult))
    E.op("vector", lambda en: en.tensor_scalar(
        out=AG[:, 1:2], in0=AG[:, 1:2], scalar1=1.0 / 1024.0, scalar2=None, op0=OP.mult))
    E.op("vector", lambda en: en.tensor_tensor(
        out=AG[:, 1:2], in0=AG[:, 1:2], in1=AG[:, 3:4], op=OP.subtract))


def ln_chunk(E, p, xap, ST, AG, XN, inc=None, waits=(), vwaits=()):
    if waits:
        E.op("vector", lambda en: en.nop(), waits=waits)
    if vwaits:
        E.op("vector", lambda en: en.nop(), waits=vwaits)
    emit_stats(E, xap, XN, AG)
    E.op("vector", lambda en: en.nop(), inc=p + "lnv")
    v = E.val(p + "lnv")
    E.op("scalar", lambda en: en.activation(AG[:, 2:3], AG[:, 1:2], AF.Ln, bias=np.float32(1e-5)),
         waits=[(p + "lnv", v)])
    E.op("scalar", lambda en: en.activation(AG[:, 2:3], AG[:, 2:3], AF.Exp, scale=np.float32(-0.5)),
         inc=p + "lns")
    s = E.val(p + "lns")
    E.op("vector", lambda en: en.tensor_scalar(
        out=XN, in0=xap, scalar1=AG[:, 0:1], scalar2=AG[:, 2:3],
        op0=OP.subtract, op1=OP.mult), waits=[(p + "lns", s)], inc=inc)



def emit_prologue(E, A):
    xr = A["x"].rearrange("s (c p) d -> p s c d", p=128)
    E.dma("sync", out=A["X"][:], in_=xr, inc="xload")
    E.dma("sync", out=A["IDENT"][:], in_=A["C_ident"][:, :], inc="cload")
    E.dma("sync", out=A["TRIU"][:], in_=A["C_triu"][:, :], inc="cload")
    E.dma("sync", out=A["B1S"][:], in_=A["L0_ffn_b1"].rearrange("(c p) -> p c", p=128),
          inc="cload", allow_slow_non_contiguous=True)
    cl = E.val("cload")

    WS = A["WSLAB"]; TO = A["TSLAB"]; SG = A["SGSLAB"]; AL = A["ALCOL"]
    ALW2 = A["ALW2"]

    for j in range(16):
        sl, hh = j // 2, j % 2
        wt = [("pwal2", E.val("pwal2") - 1)] if j >= 2 else []
        E.dma("sync", out=WS[:, j % 2, :],
              in_=A["L0_ffn_w2"][sl * 128:(sl + 1) * 128, hh * 2048:(hh + 1) * 2048],
              waits=wt, inc="pwld")
        lv = E.val("pwld")
        E.op("vector", lambda en, j=j: en.reduce_sum(
            A["ALW2H"][:, j:j + 1], WS[:, j % 2, :], axis=AX,
            apply_absolute_value=True), waits=[("pwld", lv)], inc="pwal2")
    for sl in range(8):
        E.op("vector", lambda en, sl=sl: en.tensor_tensor(
            out=ALW2[:, sl:sl + 1], in0=A["ALW2H"][:, 2 * sl:2 * sl + 1],
            in1=A["ALW2H"][:, 2 * sl + 1:2 * sl + 2], op=OP.add))

    jobs = []
    for sl in range(32):
        jobs.append((A["L0_ffn_w1"][sl * 128:(sl + 1) * 128, :],
                     A["w1t"][:, sl * 128:(sl + 1) * 128].rearrange("(a p) h -> p a h", p=128),
                     8, "solo", None, 1.0 / 1024))
    for sl in range(8):
        for hh in range(2):
            jobs.append((A["L0_ffn_w2"][sl * 128:(sl + 1) * 128, hh * 2048:(hh + 1) * 2048],
                         A["w2t"][hh * 2048:(hh + 1) * 2048, sl * 128:(sl + 1) * 128]
                         .rearrange("(a p) h -> p a h", p=128),
                         16, "col", ALW2[:, sl:sl + 1], 1.0 / 4096))
    for nm in ("Wq", "Wk", "Wv", "Wo"):
        for sl in range(8):
            jobs.append((A["L0_attn_b_" + nm][sl * 128:(sl + 1) * 128, :],
                         A[nm.lower() + "t"][:, sl * 128:(sl + 1) * 128]
                         .rearrange("(a p) h -> p a h", p=128),
                         8, "none", None, None))

    for (ji, (src, dst, nt, mode, a_ap, a_sc)) in enumerate(jobs):
        slot = ji % 2
        width = nt * 128
        wt = []
        if ji == 0:
            wt.append(("pwal2", E.val("pwal2") - 1))
        elif ji == 1:
            wt.append(("pwal2", E.val("pwal2")))
        else:
            wt.append(("pwst", 16 * (ji - 1)))
        E.dma("sync", out=WS[:, slot, 0:width], in_=src, waits=wt, inc="pwld")
        lv = E.val("pwld")
        if mode != "none":
            if mode == "solo":
                E.op("vector", lambda en, slot=slot, width=width: en.reduce_sum(
                    AL[:, 0:1], WS[:, slot, 0:width], axis=AX,
                    apply_absolute_value=True), waits=[("pwld", lv)], inc="pwal")
                aw = [("pwal", E.val("pwal"))]
                a_use = AL[:, 0:1]
            else:
                aw = []
                a_use = a_ap
            E.op("scalar", lambda en, slot=slot, width=width:
                 en.sign(SG[:, slot, 0:width], WS[:, slot, 0:width]),
                 waits=[("pwld", lv)], inc="pwsg")
            sv = E.val("pwsg")
            E.op("vector", lambda en, slot=slot, width=width, a_use=a_use, a_sc=a_sc:
                 en.tensor_scalar(out=WS[:, slot, 0:width], in0=SG[:, slot, 0:width],
                                  scalar1=a_use, scalar2=a_sc,
                                  op0=OP.mult, op1=OP.mult),
                 waits=[("pwsg", sv)] + aw, inc="pwbin")
            bw = ("pwbin", E.val("pwbin"))
        else:
            bw = ("pwld", lv)
        stv = max(0, E.val("pwst") - 16)
        for i in range(nt):
            reg = A["PALL"][:, 512 + (i % 2) * 512: 512 + (i % 2) * 512 + 128]
            wts = [bw, ("cload", cl)]
            if E.val("pwtv") >= 2:
                wts.append(("pwtv", E.val("pwtv") - 1))
            E.op("tensor", lambda en, reg=reg, slot=slot, i=i:
                 en.transpose(reg, WS[:, slot, i * 128:(i + 1) * 128], A["IDENT"][:]),
                 waits=wts, inc="pwtt")
            tv = E.val("pwtt")
            ew = [("pwtt", tv)] + ([("pwst", stv)] if i == 0 else [])
            E.op("vector", lambda en, reg=reg, i=i, slot=slot:
                 en.tensor_copy(TO[:, slot, i, :], reg),
                 waits=ew, inc="pwtv")
        ev = E.val("pwtv")
        E.dma("gpsimd", out=dst, in_=TO[:, slot, 0:nt, :],
              waits=[("pwtv", ev)], inc="pwst")
    E.op("gpsimd", lambda en: en.nop(), waits=[("pwst", E.val("pwst"))])
    E.op("sync", lambda en: en.nop(), waits=[("xload", E.val("xload"))])
    E.flush()



def emit_ffn(E, A, p):
    nc = E.nc
    X = A["X"]
    E.ffn_inst = getattr(E, "ffn_inst", 0) + 1
    tp = p + str(E.ffn_inst)
    with (
        nc.sbuf_tensor(tp + "XN", [128, 2, 1024], F32) as XN,
        nc.sbuf_tensor(tp + "XNF", [128, 8, 256], F32) as XNF,
        nc.sbuf_tensor(tp + "HS", [128, 2, 256], F32) as HS,
        nc.sbuf_tensor(tp + "W1S", [128, 2, 8, 256], F32) as W1S,
        nc.sbuf_tensor(tp + "W2S", [128, 2, 2, 1024], F32) as W2S,
        nc.sbuf_tensor(tp + "ST", [128, 2, 6], F32) as ST,
        nc.sbuf_tensor(tp + "AG", [128, 4], F32) as AG,
        nc.sbuf_tensor(tp + "MR", [128, 32], F32) as MR,
        nc.sbuf_tensor(tp + "RS", [128, 32], F32) as RS,
        nc.sbuf_tensor(tp + "B2R", [1, 1024], F32) as B2R,
    ):
        E.dma("sync", out=B2R[0:1, :], in_=A["L0_ffn_b2"].rearrange("(a d) -> a d", a=1),
              inc=p + "b2l")
        b2lv = E.val(p + "b2l")
        PALL = A["PALL"]
        PY = [PALL[:, 0:1024], PALL[:, 1024:2048]]
        PH = [PALL[:, 2048:2304], PALL[:, 2560:2816]]
        PT = PALL[:, 3072:4096]

        for gc in range(32):
            s, c = gc // 16, gc % 16
            xap = X[:, s, c, :]
            emit_stats(E, xap, XN[:, 0, :], AG)
            E.op("vector", lambda en, gc=gc: en.tensor_copy(MR[:, gc:gc + 1], AG[:, 0:1]))
            E.op("vector", lambda en, gc=gc: en.tensor_copy(RS[:, gc:gc + 1], AG[:, 1:2]),
                 inc=p + "st")
        sv = E.val(p + "st")
        E.op("scalar", lambda en: en.activation(RS[:], RS[:], AF.Ln, bias=np.float32(1e-5)),
             waits=[(p + "st", sv)])
        E.op("scalar", lambda en: en.activation(RS[:], RS[:], AF.Exp, scale=np.float32(-0.5)),
             inc=p + "rstd")
        rv = E.val(p + "rstd")

        m10 = E.val(p + "mm1")
        h0 = E.val(p + "hsf")
        l10 = E.val(p + "w1l")
        l20 = E.val(p + "w2l")

        def load_w1(k):
            wt = [(p + "mm1", m10 + 2 * (k - 1))] if k >= 2 else []
            E.dma("sync", out=W1S[:, k % 2, :, :],
                  in_=A["w1t"][:, (k % 16) * 256:(k % 16) * 256 + 256]
                  .rearrange("(a p) h -> p a h", p=128),
                  waits=wt, inc=p + "w1l")

        def load_w2(k):
            wt = [(p + "hsf", h0 + 2 * (k - 1))] if k >= 2 else []
            E.dma("sync", out=W2S[:, k % 2, :, :],
                  in_=A["w2t"][(k % 16) * 256:(k % 16) * 256 + 256, :]
                  .rearrange("(a p) d -> p a d", p=128),
                  waits=wt, inc=p + "w2l")

        load_w1(0); load_w2(0); load_w1(1); load_w2(1)

        for gi, g in enumerate(range(16)):
            s, c0 = (2 * g) // 16, (2 * g) % 16
            for lc in range(2):
                gc = 2 * g + lc
                xap = X[:, s, c0 + lc, :]
                wt = [(p + "rstd", rv)]
                if gi > 0:
                    wt.append((p + "pyfree", E.val(p + "pyfree")))
                E.op("vector", lambda en, xap=xap, lc=lc, gc=gc: en.tensor_scalar(
                    out=XN[:, lc, :], in0=xap, scalar1=MR[:, gc:gc + 1],
                    scalar2=RS[:, gc:gc + 1], op0=OP.subtract, op1=OP.mult),
                    waits=wt, inc=p + "ln")
            lnv = E.val(p + "ln")
            for i in range(16):
                lc, dc = i // 8, i % 8
                reg = PT[:, (i % 2) * 512:(i % 2) * 512 + 128]
                wt = [(p + "ln", lnv)]
                if E.val(p + "tpv") >= 1:
                    wt.append((p + "tpv", E.val(p + "tpv") - 1))
                E.op("tensor", lambda en, reg=reg, lc=lc, dc=dc:
                     en.transpose(reg, XN[:, lc, dc * 128:(dc + 1) * 128], A["IDENT"][:]),
                     waits=wt, inc=p + "tpt")
                tv = E.val(p + "tpt")
                E.op("vector", lambda en, reg=reg, lc=lc, dc=dc:
                     en.tensor_copy(XNF[:, dc, lc * 128:(lc + 1) * 128], reg),
                     waits=[(p + "tpt", tv)], inc=p + "tpv")
            xnfv = E.val(p + "tpv")

            for hc in range(32):
                k = gi * 16 + hc // 2
                if hc % 2 == 0:
                    if 2 <= k + 1 < 256:
                        load_w1(k + 1)
                    if 2 <= k + 1 < 256:
                        load_w2(k + 1)
                wt = [(p + "tpv", xnfv), (p + "w1l", l10 + 16 * (k + 1))]
                if hc >= 2:
                    wt.append((p + "act", E.val(p + "act") - 1))
                elif gi > 0:
                    wt.append((p + "act", E.val(p + "act")))
                for dc in range(8):
                    E.op("tensor", lambda en, hc=hc, dc=dc, k=k:
                         en.matmul(PH[hc % 2][:, 0:256],
                                   W1S[:, k % 2, dc, (hc % 2) * 128:(hc % 2) * 128 + 128],
                                   XNF[:, dc, :],
                                   start=(dc == 0), stop=(dc == 7)),
                         waits=(wt if dc == 0 else ()),
                         inc=(p + "mm1" if dc == 7 else None))
                m1 = E.val(p + "mm1")
                wt = [(p + "mm1", m1)]
                if hc >= 2:
                    wt.append((p + "hsf", E.val(p + "hsf") - 1))
                E.op("scalar", lambda en, hc=hc: en.activation(
                    HS[:, hc % 2, :], PH[hc % 2][:, 0:256], AF.Gelu,
                    bias=A["B1S"][:, hc:hc + 1]),
                    waits=wt, inc=p + "act")
                av = E.val(p + "act")
                wt2 = [(p + "act", av), (p + "w2l", l20 + 16 * (k + 1))]
                if gi > 0 and hc == 0:
                    wt2.append((p + "pyfree", E.val(p + "pyfree")))
                for tc in range(2):
                    for half in range(2):
                        E.op("tensor", lambda en, hc=hc, tc=tc, half=half, k=k:
                             en.matmul(PY[tc][:, half * 512:half * 512 + 512],
                                       HS[:, hc % 2, tc * 128:tc * 128 + 128],
                                       W2S[:, k % 2, hc % 2, half * 512:half * 512 + 512],
                                       start=(hc == 0), stop=False),
                             waits=(wt2 if (tc == 0 and half == 0) else ()),
                             inc=(p + "hsf" if (tc == 1 and half == 1) else None))
            for tc in range(2):
                for half in range(2):
                    E.op("tensor", lambda en, tc=tc, half=half:
                         en.matmul(PY[tc][:, half * 512:half * 512 + 512],
                                   A["TRIU"][0:1, 0:128],
                                   B2R[0:1, half * 512:half * 512 + 512],
                                   start=False, stop=True),
                         waits=([(p + "b2l", b2lv)] if (tc == 0 and half == 0) else ()),
                         inc=(p + "mm2d" if (tc == 1 and half == 1) else None))
            m2 = E.val(p + "mm2d")
            for tc in range(2):
                E.op("vector", lambda en, tc=tc, s=s, c0=c0: en.tensor_tensor(
                    out=X[:, s, c0 + tc, :], in0=X[:, s, c0 + tc, :],
                    in1=PY[tc][:, :], op=OP.add),
                    waits=[(p + "mm2d", m2)],
                    inc=(p + "pyfree" if tc == 1 else None))
        E.flush()



def emit_attn_b(E, A):
    nc = E.nc
    X = A["X"]
    PALL = A["PALL"]
    PA = PALL[:, 0:1024]
    PB = PALL[:, 1024:2048]
    PC = PALL[:, 2048:3072]
    PD = PALL[:, 3072:4096]
    with ExitStack() as stk:
        XNYF = stk.enter_context(nc.sbuf_tensor("bXNYF", [128, 1024], F32))
        XNFPIV = stk.enter_context(nc.sbuf_tensor("bXNFPIV", [128, 1024], F32))
        AG = stk.enter_context(nc.sbuf_tensor("bAG", [128, 4], F32))
        QF = stk.enter_context(nc.sbuf_tensor("bQF", [128, 1024], F32))
        KT = stk.enter_context(nc.sbuf_tensor("bKT", [128, 1024], F32))
        SS = stk.enter_context(nc.sbuf_tensor("bSS", [128, 4], F32))
        MTS = stk.enter_context(nc.sbuf_tensor("bMTS", [128, 256], F32))
        RN = stk.enter_context(nc.sbuf_tensor("bRN", [128, 2], F32))
        GSLSB = stk.enter_context(nc.sbuf_tensor("bGSLSB", [128, 1024], F32))
        GAMKH = stk.enter_context(nc.sbuf_tensor("bGAMKH", [128, 1024], F32))
        LAMPF = stk.enter_context(nc.sbuf_tensor("bLAMPF", [128, 1024], F32))
        RHOPCB = stk.enter_context(nc.sbuf_tensor("bRHOPCB", [128, 1024], F32))
        LGKTL = stk.enter_context(nc.sbuf_tensor("bLGKTL", [128, 1024], F32))

        S0 = stk.enter_context(nc.sbuf_tensor("bS0", [128, 2048], F32))
        S1 = stk.enter_context(nc.sbuf_tensor("bS1", [128, 2048], F32))
        WBS = stk.enter_context(nc.sbuf_tensor("bWBS", [128, 2, 1024], F32))
        GPB = stk.enter_context(nc.sbuf_tensor("bGPB", [128, 4096], F32))
        XN = XNYF; YF = XNYF; VT = XNYF
        PCR = GAMKH[0:1, 0:1024]
        VF = XNFPIV
        XNF = XNFPIV
        PIV = XNFPIV
        GS = GSLSB; LSB = GSLSB
        GAM = GAMKH; KH = GAMKH
        LAM = LAMPF; PF = LAMPF
        RHO = RHOPCB; PCB = RHOPCB
        LG = LGKTL; KTL = LGKTL
        SH = [S0, S1]

        pr = [("L0_attn_b_W_gamma", 0), ("L0_attn_b_b_gamma", 1),
              ("L0_attn_b_W_lambda", 2), ("L0_attn_b_b_lambda", 3)]
        for (nm, row) in pr:
            wt = [("bgpt", E.val("bgpt"))] if row > 0 else []
            for h in range(2):
                E.dma("sync", out=PCR[0:1, h * 512:(h + 1) * 512],
                      in_=A[nm][h:h + 1, :], waits=wt, inc="bprl")
                wt = []
            plv = E.val("bprl")
            for half in range(2):
                wt = [("bgpv", E.val("bgpv"))]
                if half == 0:
                    wt.append(("bprl", plv))
                E.op("tensor", lambda en, half=half:
                     en.matmul(PD[:, 0:512], A["TRIU"][0:1, 0:128],
                               PCR[0:1, half * 512:half * 512 + 512],
                               start=True, stop=True),
                     waits=wt, inc="bgpt")
                tv = E.val("bgpt")
                E.op("vector", lambda en, row=row, half=half:
                     en.tensor_copy(GPB[:, row * 1024 + half * 512:row * 1024 + half * 512 + 512],
                                    PD[:, 0:512]),
                     waits=[("bgpt", tv)], inc="bgpv")
        gpv = E.val("bgpv")

        nld = [0]
        sched = []

        def jit(g):
            while nld[0] <= min(g + 1, len(sched) - 1):
                k = nld[0]
                which, dc = sched[k]
                wt = []
                if k >= 2:
                    wt.append(("bwf", min(k - 1, E.val("bwf"))))
                E.dma("sync", out=WBS[:, k % 2, :],
                      in_=A[which][dc * 128:(dc + 1) * 128, :], waits=wt, inc="bwl")
                nld[0] += 1

        for s in range(SPC):
            w0 = [("bgpv", gpv)]
            if s > 0:
                w0.append(("byprd", E.val("byprd")))
            E.op("vector", lambda en: en.memset(S0[:], 0.0), waits=w0)
            E.op("vector", lambda en: en.memset(S1[:], 0.0), inc="bsup")
            for c in range(NCH):
                it = s * NCH + c
                base = it * 32
                sched.extend([("wqt", dc) for dc in range(8)])
                sched.extend([("wkt", dc) for dc in range(8)])
                sched.extend([("wvt", dc) for dc in range(8)])
                sched.extend([("wot", dc) for dc in range(8)])
                xap = X[:, s, c, :]
                lnw = [("bwf", it * 32)] if it > 0 else []
                ln_chunk(E, "b", xap, None, AG, XN[:], inc="bln", vwaits=lnw)
                lnv = E.val("bln")
                sup = E.val("bsup")
                for i in range(8):
                    reg = PD[:, (i % 2) * 512:(i % 2) * 512 + 128]
                    wt = [("bln", lnv)]
                    if i < 2:
                        wt.append(("bsup", sup))
                    else:
                        wt.append(("btev", E.val("btev") - 1))
                    E.op("tensor", lambda en, reg=reg, i=i:
                         en.transpose(reg, XN[:, i * 128:(i + 1) * 128], A["IDENT"][:]),
                         waits=wt, inc="btet")
                    tv = E.val("btet")
                    ew = [("btet", tv)]
                    if i == 0 and it > 0:
                        ew.append(("bktl", E.val("bktl")))
                    E.op("vector", lambda en, reg=reg, i=i:
                         en.tensor_copy(XNF[:, i * 128:(i + 1) * 128], reg),
                         waits=ew, inc="btev")
                xnfv = E.val("btev")
                for dc in range(8):
                    jit(base + dc)
                    wq = [("btev", xnfv), ("bwl", 16 * (base + dc + 1))]
                    if it > 0 and dc == 0:
                        wq.append(("byfv", E.val("byfv")))
                    for dkc in range(8):
                        E.op("tensor", lambda en, dc=dc, dkc=dkc, sl=(base + dc):
                             en.matmul(PA[:, dkc * 128:dkc * 128 + 128],
                                       WBS[:, sl % 2, dkc * 128:dkc * 128 + 128],
                                       XNF[:, dc * 128:dc * 128 + 128],
                                       start=(dc == 0 and dkc % 4 == 0),
                                       stop=(dc == 7 and dkc % 4 == 3)),
                             waits=(wq if dkc == 0 else ()),
                             inc=("bwf" if dkc == 7 else None))
                E.op("vector", lambda en: en.tensor_copy(QF[:, 0:512], PA[:, 0:512]),
                     waits=[("bwf", base + 8)])
                E.op("vector", lambda en: en.tensor_copy(QF[:, 512:1024], PA[:, 512:1024]),
                     inc="bqev")
                qev = E.val("bqev")
                for dc in range(8):
                    jit(base + 8 + dc)
                    wk = [("bwl", 16 * (base + 8 + dc + 1))]
                    if it > 0 and dc == 0:
                        wk.append(("bres", E.val("bres")))
                    for half in range(2):
                        E.op("tensor", lambda en, dc=dc, half=half, sl=(base + 8 + dc):
                             en.matmul(PB[:, half * 512:half * 512 + 512],
                                       XNF[:, dc * 128:dc * 128 + 128],
                                       WBS[:, sl % 2, half * 512:half * 512 + 512],
                                       start=(dc == 0), stop=(dc == 7)),
                             waits=(wk if half == 0 else ()),
                             inc=("bwf" if half == 1 else None))
                E.op("vector", lambda en: en.tensor_copy(KT[:, 0:512], PB[:, 0:512]),
                     waits=[("bwf", base + 16)])
                E.op("vector", lambda en: en.tensor_copy(KT[:, 512:1024], PB[:, 512:1024]),
                     inc="bkev")
                E.op("vector", lambda en: en.tensor_tensor(
                    out=GS[:], in0=KT[:], in1=KT[:], op=OP.mult),
                    waits=[("btes", E.val("btes"))] if it > 0 else [])
                for h in range(2):
                    E.op("vector", lambda en, h=h: en.reduce_sum(
                        SS[:, h:h + 1], GS[:, h * 512:h * 512 + 512], axis=AX))
                E.op("vector", lambda en: en.tensor_scalar(
                    out=SS[:, 2:4], in0=SS[:, 0:2], scalar1=1e-20, scalar2=None,
                    op0=OP.max), inc="bssv")
                ssv = E.val("bssv")
                E.op("scalar", lambda en: en.activation(RN[:], SS[:, 2:4], AF.Ln),
                     waits=[("bssv", ssv)])
                E.op("scalar", lambda en: en.activation(RN[:], RN[:], AF.Exp, scale=np.float32(-0.5)),
                     inc="brnv")
                rnv = E.val("brnv")
                for h in range(2):
                    E.op("vector", lambda en, h=h: en.tensor_scalar(
                        out=KT[:, h * 512:h * 512 + 512], in0=KT[:, h * 512:h * 512 + 512],
                        scalar1=RN[:, h:h + 1], scalar2=None, op0=OP.mult),
                        waits=([("brnv", rnv)] if h == 0 else ()),
                        inc=("bkhat" if h == 1 else None))
                khv = E.val("bkhat")
                for dc in range(8):
                    jit(base + 16 + dc)
                    wv = [("bwl", 16 * (base + 16 + dc + 1))]
                    if it > 0 and dc == 0:
                        wv.append(("bpcbv", E.val("bpcbv")))
                        wv.append(("bmtsv", E.val("bmtsv")))
                    for half in range(2):
                        E.op("tensor", lambda en, dc=dc, half=half, sl=(base + 16 + dc):
                             en.matmul(PC[:, half * 512:half * 512 + 512],
                                       XNF[:, dc * 128:dc * 128 + 128],
                                       WBS[:, sl % 2, half * 512:half * 512 + 512],
                                       start=(dc == 0), stop=(dc == 7)),
                             waits=(wv if half == 0 else ()),
                             inc=("bwf" if half == 1 else None))
                E.op("vector", lambda en: en.tensor_copy(VT[:, 0:512], PC[:, 0:512]),
                     waits=[("bwf", base + 24)])
                E.op("vector", lambda en: en.tensor_copy(VT[:, 512:1024], PC[:, 512:1024]),
                     inc="bvev")
                vev = E.val("bvev")
                for i in range(8):
                    reg = PD[:, (i % 2) * 512:(i % 2) * 512 + 128]
                    wt = [("bvev", vev), ("btev", E.val("btev") - 1)]
                    E.op("tensor", lambda en, reg=reg, i=i:
                         en.transpose(reg, VT[:, i * 128:(i + 1) * 128], A["IDENT"][:]),
                         waits=wt, inc="btet")
                    tv = E.val("btet")
                    ew = [("btet", tv)]
                    if i == 0:
                        ew.append(("bktl", E.val("bktl")))
                    E.op("vector", lambda en, reg=reg, i=i:
                         en.tensor_copy(VF[:, i * 128:(i + 1) * 128], reg),
                         waits=ew, inc="btev")
                vfv = E.val("btev")
                for h in range(2):
                    for vc in range(4):
                        j = h * 4 + vc
                        E.op("tensor", lambda en, h=h, j=j:
                             en.matmul(PC[:, h * 128:h * 128 + 128],
                                       VF[:, j * 128:j * 128 + 128],
                                       QF[:, j * 128:j * 128 + 128],
                                       start=(j % 4 == 0), stop=(j % 4 == 3)),
                             waits=([("btev", vfv), ("bqev", qev),
                                     ("bpcbv", E.val("bpcbv"))] if j == 0 else ()),
                             inc=("bgrd" if j == 7 else None))
                grd = E.val("bgrd")
                for h in range(2):
                    E.op("vector", lambda en, h=h: en.tensor_tensor(
                        out=MTS[:, h * 128:h * 128 + 128], in0=PC[:, h * 128:h * 128 + 128],
                        in1=A["TRIU"][:], op=OP.mult),
                        waits=([("bgrd", grd)] if h == 0 else ()),
                        inc=("bmtsv" if h == 1 else None))
                mtsv = E.val("bmtsv")
                E.op("vector", lambda en: en.tensor_tensor(
                    out=GS[:], in0=KT[:], in1=GPB[:, 0:1024], op=OP.mult),
                    waits=[("bkhat", khv)])
                E.op("vector", lambda en: en.tensor_tensor(
                    out=GS[:], in0=GS[:], in1=GPB[:, 1024:2048], op=OP.add),
                    inc="bzg")
                zg = E.val("bzg")
                gw = [("bzg", zg)]
                if it > 0:
                    gw.append(("bsud", E.val("bsud")))
                E.op("scalar", lambda en: en.activation(GAM[:], GS[:], AF.Exp, scale=np.float32(-1.0)),
                     waits=gw, inc="bze")
                ze = E.val("bze")
                E.op("vector", lambda en: en.tensor_scalar(
                    out=GAM[:], in0=GAM[:], scalar1=1.0, scalar2=None, op0=OP.add),
                    waits=[("bze", ze)])
                E.op("vector", lambda en: en.reciprocal(GAM[:], GAM[:]))
                E.op("vector", lambda en: en.tensor_tensor(
                    out=GS[:], in0=KT[:], in1=GPB[:, 2048:3072], op=OP.mult),
                    waits=[("bze", ze)])
                E.op("vector", lambda en: en.tensor_tensor(
                    out=GS[:], in0=GS[:], in1=GPB[:, 3072:4096], op=OP.add),
                    inc="bzl")
                zl = E.val("bzl")
                lw = [("bzl", zl)]
                if it > 0:
                    lw.append(("byfv", E.val("byfv")))
                E.op("scalar", lambda en: en.activation(LAM[:], GS[:], AF.Exp, scale=np.float32(-1.0)),
                     waits=lw, inc="bzle")
                zle = E.val("bzle")
                E.op("vector", lambda en: en.tensor_scalar(
                    out=LAM[:], in0=LAM[:], scalar1=1.0, scalar2=None, op0=OP.add),
                    waits=[("bzle", zle)])
                E.op("vector", lambda en: en.reciprocal(LAM[:], LAM[:]))
                E.op("vector", lambda en: en.tensor_scalar(
                    out=RHO[:], in0=LAM[:], scalar1=-1.0, scalar2=1.0,
                    op0=OP.mult, op1=OP.add))
                E.op("vector", lambda en: en.tensor_tensor(
                    out=RHO[:], in0=RHO[:], in1=GAM[:], op=OP.mult), inc="brho")
                rhov = E.val("brho")
                lgw = [("brho", rhov)]
                if it > 0:
                    lgw.append(("byprd", E.val("byprd")))
                E.op("scalar", lambda en: en.activation(LG[:], RHO[:], AF.Ln),
                     waits=lgw, inc="blg")
                lgv = E.val("blg")
                for half in range(2):
                    E.op("tensor", lambda en, half=half:
                         en.matmul(PB[:, half * 512:half * 512 + 512],
                                   A["TRIU"][:], LG[:, half * 512:half * 512 + 512],
                                   start=True, stop=True),
                         waits=([("blg", lgv), ("bkev", E.val("bkev"))] if half == 0 else ()),
                         inc=("bLd" if half == 1 else None))
                ld = E.val("bLd")
                E.op("vector", lambda en: en.tensor_copy(LSB[:], PB[:]),
                     waits=[("bLd", ld)], inc="blsb")
                lsbv = E.val("blsb")
                E.op("scalar", lambda en: en.activation(PIV[:], PB[:], AF.Exp, scale=np.float32(-1.0)),
                     waits=[("bLd", ld), ("bgrd", grd), ("blsb", lsbv)])
                for half in range(2):
                    E.op("tensor", lambda en, half=half:
                         en.matmul(PD[0:1, half * 512:half * 512 + 512],
                                   A["IDENT"][:, 127:128],
                                   LSB[:, half * 512:half * 512 + 512],
                                   start=True, stop=True),
                         waits=([("blsb", lsbv), ("btev", E.val("btev"))]
                                if half == 0 else ()),
                         inc=("bpcl" if half == 1 else None))
                pclv = E.val("bpcl")
                E.op("scalar", lambda en: en.activation(
                    PCR[0:1, :], PD[0:1, 0:1024], AF.Exp),
                     waits=[("bpcl", pclv)], inc="bpcr")
                pcrv = E.val("bpcr")
                E.op("vector", lambda en: en.tensor_scalar(
                    out=KTL[:], in0=RHO[:], scalar1=-1.0, scalar2=1.0,
                    op0=OP.mult, op1=OP.add), waits=[("bLd", ld)])
                E.op("vector", lambda en: en.tensor_tensor(
                    out=KTL[:], in0=KTL[:], in1=KT[:], op=OP.mult))
                E.op("vector", lambda en: en.tensor_tensor(
                    out=KTL[:], in0=KTL[:], in1=PIV[:], op=OP.mult),
                    waits=[("bpcr", pcrv)], inc="bktl")
                ktlv = E.val("bktl")
                for half in range(2):
                    E.op("tensor", lambda en, half=half:
                         en.matmul(PC[:, half * 512:half * 512 + 512],
                                   A["TRIU"][0:1, 0:128], PCR[0:1, half * 512:half * 512 + 512],
                                   start=True, stop=True),
                         waits=([("bpcr", pcrv), ("bvev", vev), ("bmtsv", mtsv)]
                                if half == 0 else ()),
                         inc=("bpcd" if half == 1 else None))
                pcd = E.val("bpcd")
                E.op("vector", lambda en: en.tensor_copy(PCB[:], PC[:]),
                     waits=[("bpcd", pcd), ("bktl", ktlv)], inc="bpcbv")
                E.op("vector", lambda en: en.tensor_tensor(
                    out=KH[:], in0=KTL[:], in1=PCB[:], op=OP.mult), inc="bkh")
                khx = E.val("bkh")
                for i in range(8):
                    reg = PD[:, (i % 2) * 512:(i % 2) * 512 + 128]
                    if i >= 2:
                        wt = [("blsb", lsbv), ("btes", E.val("btes") - 1)]
                    else:
                        wt = [("blsb", lsbv), ("btev", E.val("btev")),
                              ("bpcr", pcrv)]
                    E.op("tensor", lambda en, reg=reg, i=i:
                         en.transpose(reg, LSB[:, i * 128:(i + 1) * 128], A["IDENT"][:]),
                         waits=wt, inc="btet")
                    tv = E.val("btet")
                    E.op("scalar", lambda en, reg=reg, i=i:
                         en.activation(PF[:, i * 128:(i + 1) * 128], reg, AF.Exp),
                         waits=[("btet", tv)], inc="btes")
                pfv = E.val("btes")
                for h in range(2):
                    for dkc in range(4):
                        blk = PA[:, (h * 4 + dkc) * 128:(h * 4 + dkc) * 128 + 128]
                        E.op("tensor", lambda en, h=h, dkc=dkc, blk=blk:
                             en.matmul(blk, KTL[:, h * 512 + dkc * 128:h * 512 + dkc * 128 + 128],
                                       MTS[:, h * 128:h * 128 + 128],
                                       start=True, stop=False),
                             waits=([("bmtsv", mtsv), ("bktl", ktlv), ("bqev", qev),
                                     ("bsup", E.val("bsup"))] if (h == 0 and dkc == 0) else ()))
                        for vc in range(4):
                            E.op("tensor", lambda en, h=h, dkc=dkc, vc=vc, blk=blk:
                                 en.matmul(blk, SH[h][:, vc * 512 + dkc * 128:vc * 512 + dkc * 128 + 128],
                                           QF[:, (h * 4 + vc) * 128:(h * 4 + vc) * 128 + 128],
                                           start=False, stop=(vc == 3)),
                                 inc=("byprd" if (h == 1 and dkc == 3 and vc == 3) else None))
                yprd = E.val("byprd")
                for h in range(2):
                    for vc in range(4):
                        i = h * 4 + vc
                        reg = PD[:, (i % 2) * 512:(i % 2) * 512 + 512]
                        wt = [("bkh", khx), ("btes", pfv)]
                        if i >= 2:
                            wt.append(("bsup", E.val("bsup") - 1))
                        E.op("tensor", lambda en, h=h, vc=vc, reg=reg:
                             en.matmul(reg, VT[:, h * 512 + vc * 128:h * 512 + vc * 128 + 128],
                                       KH[:, h * 512:h * 512 + 512],
                                       start=True, stop=True),
                             waits=wt, inc="bsud")
                        sud = E.val("bsud")
                        E.op("vector", lambda en, h=h, vc=vc: en.tensor_tensor(
                            out=SH[h][:, vc * 512:vc * 512 + 512],
                            in0=SH[h][:, vc * 512:vc * 512 + 512],
                            in1=PCB[:, h * 512:h * 512 + 512], op=OP.mult),
                            waits=([("byprd", yprd)] if i == 0 else ()))
                        E.op("vector", lambda en, h=h, vc=vc, reg=reg: en.tensor_tensor(
                            out=SH[h][:, vc * 512:vc * 512 + 512],
                            in0=SH[h][:, vc * 512:vc * 512 + 512],
                            in1=reg, op=OP.add),
                            waits=[("bsud", sud)], inc="bsup")
                sudall = E.val("bsud")
                E.op("vector", lambda en: en.tensor_tensor(
                    out=YF[:], in0=PA[:], in1=PF[:], op=OP.mult),
                    waits=[("byprd", yprd), ("btes", pfv), ("bsud", sudall)], inc="byfv")
                yfv = E.val("byfv")
                for dyc in range(8):
                    jit(base + 24 + dyc)
                    ww = [("bwl", 16 * (base + 24 + dyc + 1)), ("byfv", yfv)]
                    if dyc == 0:
                        ww.append(("blsb", lsbv))
                        ww.append(("bpcr", pcrv))
                    for half in range(2):
                        E.op("tensor", lambda en, dyc=dyc, half=half, sl=(base + 24 + dyc):
                             en.matmul(PB[:, half * 512:half * 512 + 512],
                                       YF[:, dyc * 128:dyc * 128 + 128],
                                       WBS[:, sl % 2, half * 512:half * 512 + 512],
                                       start=(dyc == 0), stop=(dyc == 7)),
                             waits=(ww if half == 0 else ()),
                             inc=("bwf" if half == 1 else None))
                E.op("vector", lambda en, s=s, c=c: en.tensor_tensor(
                    out=X[:, s, c, :], in0=X[:, s, c, :], in1=PB[:], op=OP.add),
                    waits=[("bwf", base + 32)], inc="bres")
        E.flush()



def emit_attn_a(E, A):
    nc = E.nc
    X = A["X"]
    PALL = A["PALL"]
    PPR = PALL
    PGR = PALL[:, 0:1024]
    PS = PALL[:, 1024:2048]
    PH = PALL[:, 2048:3072]
    PT = PALL[:, 3072:4096]
    subs, var_of = ACONSTS["subs"], ACONSTS["var_of"]
    with ExitStack() as stk:
        KF = stk.enter_context(nc.sbuf_tensor("aKF", [128, 8, 384], F32))
        QF = stk.enter_context(nc.sbuf_tensor("aQF", [128, 8, 384], F32))
        VTM = stk.enter_context(nc.sbuf_tensor("aVTM", [128, 3, 1024], F32))
        OF = stk.enter_context(nc.sbuf_tensor("aOF", [128, 1024], F32))
        XNA = stk.enter_context(nc.sbuf_tensor("aXNA", [128, 1024], F32))
        XNFA = stk.enter_context(nc.sbuf_tensor("aXNFA", [128, 1024], F32))
        ZSB = stk.enter_context(nc.sbuf_tensor("aZSB", [128, 1024], F32))
        SPS = stk.enter_context(nc.sbuf_tensor("aSPS", [64, 1024], F32))
        WQT = stk.enter_context(nc.sbuf_tensor("aWQT", [128, 2, 1024], F32))
        WOS = stk.enter_context(nc.sbuf_tensor("aWOS", [128, 2, 512], F32))
        CMK = stk.enter_context(nc.sbuf_tensor("aCMK", [128, 3, QW], F32))
        CSL = stk.enter_context(nc.sbuf_tensor("aCSL", [128, 3, 64], F32))
        CSLT = stk.enter_context(nc.sbuf_tensor("aCSLT", [64, 3, 128], F32))
        CM2 = stk.enter_context(nc.sbuf_tensor("aCM2", [64, 3, QW], F32))
        AEDG = stk.enter_context(nc.sbuf_tensor("aAEDG", [1, 3, 128], F32))
        BOB2 = stk.enter_context(nc.sbuf_tensor("aBOB2", [1, 1024], F32))
        AG = stk.enter_context(nc.sbuf_tensor("aAG", [128, 4], F32))
        WTS = XNA
        ZP = XNFA
        VWIN = XNFA
        PYOS = ZSB
        BOR = BOB2[0:1, :]
        ST = None
        E.dma("sync", out=CMK[:], in_=A["CA_mask"].rearrange("v s t -> s v t"), inc="acl")
        E.dma("sync", out=CSL[:], in_=A["CA_sel"].rearrange("v s n -> s v n"), inc="acl")
        E.dma("sync", out=CSLT[:], in_=A["CA_selT"].rearrange("v n s -> n v s"), inc="acl")
        E.dma("sync", out=CM2[:], in_=A["CA_m2"].rearrange("v n t -> n v t"), inc="acl")
        E.dma("sync", out=AEDG[:], in_=A["CA_edge"].rearrange("v a t -> a v t"), inc="acl")
        E.dma("sync", out=BOR, in_=A["L0_attn_a_b_out"].rearrange("(a d) -> a d", a=1),
              inc="acl")
        aclv = E.val("acl")
        bo2v = 0

        nql = [0]
        qsched = []

        def qjit(k_need):
            while nql[0] <= min(k_need + 1, len(qsched) - 1):
                k = nql[0]
                dc, third = qsched[k]
                wt = []
                if k >= 2:
                    wt.append(("aqf", min(k - 1, E.val("aqf"))))
                E.dma("sync", out=WQT[:, k % 2, :],
                      in_=A["L0_attn_a_w_qkv"][dc * 128:(dc + 1) * 128,
                                               third * 1024:(third + 1) * 1024],
                      waits=wt, inc="aql")
                nql[0] += 1

        nol = [0]
        osched = []

        def ojit(k_need):
            while nol[0] <= min(k_need + 1, len(osched) - 1):
                k = nol[0]
                dc, half = osched[k]
                wt = []
                if k >= 2:
                    wt.append(("awof", min(k - 1, E.val("awof"))))
                E.dma("sync", out=WOS[:, k % 2, :],
                      in_=A["L0_attn_a_w_out"][dc * 128:(dc + 1) * 128,
                                               half * 512:(half + 1) * 512],
                      waits=wt, inc="awol")
                nol[0] += 1

        nacc = [0]

        for s in range(SPC):
            g_of = []
            for (j, (tq, w, s0, sw, n0, nw)) in enumerate(subs):
                gj = max(0, -(-((s0 + sw) - 384) // 256))
                assert 256 * gj <= s0 and (s0 + sw) <= 256 * gj + 384 and gj <= 7
                g_of.append(gj)
            for g in range(8):
                if g == 0:
                    slots = [0, 1, 2]
                elif g == 7:
                    slots = [1]
                else:
                    slots = [1, 2]
                if g > 0:
                    shw = [("agrd", E.val("agrd")), ("aofd", E.val("aofd"))]
                    E.op("vector", lambda en: en.tensor_copy(
                        KF[:, :, 0:128], KF[:, :, 256:384]), waits=shw)
                    E.op("vector", lambda en: en.tensor_copy(
                        QF[:, :, 0:128], QF[:, :, 256:384]))
                    E.op("vector", lambda en: en.tensor_copy(
                        VTM[:, 0, :], VTM[:, 2, :]),
                        waits=[("avwd", E.val("avwd"))], inc="ashf")
                shfv = E.val("ashf")
                for sl in slots:
                    c = 2 * g + sl
                    lnw = [("aofd", E.val("aofd")), ("acl", aclv)]
                    ln_chunk(E, "a", X[:, s, c, :], ST, AG, XNA[:], inc="aln",
                             vwaits=lnw)
                    lnv = E.val("aln")
                    for i in range(8):
                        reg = PT[:, (i % 2) * 512:(i % 2) * 512 + 128]
                        wt = [("aln", lnv)]
                        if E.val("atpv") >= 1:
                            wt.append(("atpv", E.val("atpv") - 1))
                        if i < 2:
                            wt.append(("aofe", E.val("aofe")))
                        E.op("tensor", lambda en, reg=reg, i=i:
                             en.transpose(reg, XNA[:, i * 128:(i + 1) * 128], A["IDENT"][:]),
                             waits=wt, inc="atpt")
                        tv = E.val("atpt")
                        ew = [("atpt", tv)]
                        if i == 0:
                            ew.append(("azex", E.val("azex")))
                            ew.append(("avwd", E.val("avwd")))
                        E.op("vector", lambda en, reg=reg, i=i:
                             en.tensor_copy(XNFA[:, i * 128:(i + 1) * 128], reg),
                             waits=ew, inc="atpv")
                    xnfv = E.val("atpv")
                    base = len(qsched)
                    qsched.extend([(dc, th) for dc in range(8) for th in range(3)])
                    for dc in range(8):
                        for th in range(3):
                            k = base + dc * 3 + th
                            qjit(k)
                            wt = [("atpv", xnfv), ("aql", 16 * (k + 1))]
                            if dc == 0 and th == 0:
                                wt.append(("apye", E.val("apye")))
                                wt.append(("aspr", E.val("aspr")))
                                wt.append(("awtv", E.val("awtv")))
                            if th < 2:
                                for oc in range(8):
                                    blk = th * 8 + oc
                                    E.op("tensor", lambda en, dc=dc, oc=oc, blk=blk, k=k:
                                         en.matmul(PPR[:, blk * 128:blk * 128 + 128],
                                                   WQT[:, k % 2, oc * 128:oc * 128 + 128],
                                                   XNFA[:, dc * 128:dc * 128 + 128],
                                                   start=(dc == 0 and blk % 4 == 0),
                                                   stop=(dc == 7 and blk % 4 == 3)),
                                         waits=(wt if oc == 0 else ()),
                                         inc=("aqf" if oc == 7 else None))
                            else:
                                for half in range(2):
                                    E.op("tensor", lambda en, dc=dc, half=half, k=k:
                                         en.matmul(PPR[:, 2048 + half * 512:2048 + half * 512 + 512],
                                                   XNFA[:, dc * 128:dc * 128 + 128],
                                                   WQT[:, k % 2, half * 512:half * 512 + 512],
                                                   start=(dc == 0), stop=(dc == 7)),
                                         waits=(wt if half == 0 else ()),
                                         inc=("aqf" if half == 1 else None))
                    E.op("tensor", lambda en: en.nop(), waits=[("aqf", E.val("aqf"))],
                         inc="aprd")
                    pv = E.val("aprd")
                    for oc in range(8):
                        E.op("vector", lambda en, oc=oc, sl=sl: en.tensor_copy(
                            QF[:, oc, sl * 128:sl * 128 + 128],
                            PPR[:, oc * 128:oc * 128 + 128]),
                            waits=([("aprd", pv)] if oc == 0 else ()))
                    for oc in range(8):
                        E.op("vector", lambda en, oc=oc, sl=sl: en.tensor_copy(
                            KF[:, oc, sl * 128:sl * 128 + 128],
                            PPR[:, 1024 + oc * 128:1024 + oc * 128 + 128]))
                    for half in range(2):
                        E.op("vector", lambda en, half=half, sl=sl: en.tensor_copy(
                            VTM[:, sl, half * 512:half * 512 + 512],
                            PPR[:, 2048 + half * 512:2048 + half * 512 + 512]),
                            inc=("apev" if half == 1 else None))
                pev = E.val("apev")
                for (j, gj) in enumerate(g_of):
                    if gj != g:
                        continue
                    tq, w, s0, sw, n0, nw = subs[j]
                    v = var_of[j]
                    sl0 = s0 - 256 * g
                    tq0 = tq - 256 * g
                    for h in range(8):
                        wt = []
                        if h == 0:
                            wt = [("apev", pev), ("ashf", shfv),
                                  ("apye", E.val("apye"))]
                        E.op("tensor", lambda en, h=h, sl0=sl0, sw=sw, tq0=tq0, w=w:
                             en.matmul(PGR[0:sw, h * 128:h * 128 + w],
                                       KF[:, h, sl0:sl0 + sw], QF[:, h, tq0:tq0 + w],
                                       start=True, stop=True),
                             waits=wt, inc=("agrd" if h == 7 else None))
                    grd = E.val("agrd")
                    for h in range(8):
                        wt = [("agrd", grd)] if h == 0 else []
                        E.op("vector", lambda en, h=h, sw=sw, w=w, v=v:
                             en.scalar_tensor_tensor(
                                 out=ZP[0:sw, h * 128:h * 128 + w],
                                 in0=PGR[0:sw, h * 128:h * 128 + w],
                                 scalar=1.0 / 32.0,
                                 in1=CMK[0:sw, v, 0:w],
                                 op0=OP.mult, op1=OP.add),
                             waits=wt, inc=("azs" if h == 7 else None))
                    zsv = E.val("azs")
                    aw = [("azs", zsv), ("aacc", 16 * nacc[0])]
                    for h in range(8):
                        E.op("scalar", lambda en, h=h, sw=sw, w=w:
                             en.activation(ZSB[0:sw, h * 128:h * 128 + w],
                                           ZP[0:sw, h * 128:h * 128 + w], AF.Exp),
                             waits=(aw if h == 0 else ()),
                             inc=("azex" if h == 7 else None))
                    zev = E.val("azex")
                    for h in range(8):
                        wt = []
                        if h == 0:
                            wt = [("azex", zev), ("aspr", E.val("aspr"))]
                        E.op("tensor", lambda en, h=h, sw=sw, w=w, v=v, nw=nw:
                             en.matmul(PS[0:nw, h * 128:h * 128 + w],
                                       CSL[0:sw, v, 0:nw], ZSB[0:sw, h * 128:h * 128 + w],
                                       start=True, stop=True),
                             waits=wt, inc=("asd" if h == 7 else None))
                    sdv = E.val("asd")
                    for h in range(8):
                        wt = [("asd", sdv)] if h == 0 else []
                        E.op("vector", lambda en, h=h, nw=nw, w=w:
                             en.tensor_scalar(out=SPS[0:nw, h * 128:h * 128 + w],
                                              in0=PS[0:nw, h * 128:h * 128 + w],
                                              scalar1=1.0, scalar2=None, op0=OP.add),
                             waits=wt)
                        E.op("vector", lambda en, h=h, nw=nw, w=w, v=v:
                             en.tensor_tensor(out=SPS[0:nw, h * 128:h * 128 + w],
                                              in0=SPS[0:nw, h * 128:h * 128 + w],
                                              in1=CM2[0:nw, v, 0:w], op=OP.subtract))
                        E.op("vector", lambda en, h=h, nw=nw, w=w:
                             en.reciprocal(SPS[0:nw, h * 128:h * 128 + w],
                                           SPS[0:nw, h * 128:h * 128 + w]))
                        E.op("vector", lambda en, h=h, nw=nw, w=w, v=v:
                             en.tensor_tensor(out=SPS[0:nw, h * 128:h * 128 + w],
                                              in0=SPS[0:nw, h * 128:h * 128 + w],
                                              in1=CM2[0:nw, v, 0:w], op=OP.mult),
                             inc=("aspr" if h == 7 else None))
                    spv = E.val("aspr")
                    for h in range(8):
                        wt = []
                        if h == 0:
                            wt = [("aspr", spv), ("awtv", E.val("awtv"))]
                        E.op("tensor", lambda en, h=h, sw=sw, w=w, v=v, nw=nw:
                             en.matmul(PH[0:sw, h * 128:h * 128 + w],
                                       CSLT[0:nw, v, 0:sw], SPS[0:nw, h * 128:h * 128 + w],
                                       start=True, stop=True),
                             waits=wt, inc=("ahd" if h == 7 else None))
                    hdv = E.val("ahd")
                    for h in range(8):
                        wt = [("ahd", hdv)] if h == 0 else []
                        E.op("vector", lambda en, h=h, sw=sw, w=w:
                             en.tensor_tensor(out=WTS[0:sw, h * 128:h * 128 + w],
                                              in0=ZSB[0:sw, h * 128:h * 128 + w],
                                              in1=PH[0:sw, h * 128:h * 128 + w], op=OP.mult),
                             waits=wt, inc=("awtv" if h == 7 else None))
                    wtv = E.val("awtv")
                    pieces = []
                    r = 0
                    while r < sw:
                        sabs = s0 + r
                        cs = sabs // 128
                        p0 = sabs % 128
                        run = min(sw - r, 128 - p0)
                        pieces.append((cs - 2 * g, p0, r, run))
                        r += run
                    vww = [("azex", zev), ("apev", pev), ("ashf", shfv)]
                    for (slot, p0, r0, run) in pieces:
                        E.dma("gpsimd", out=VWIN[r0:r0 + run, :],
                              in_=VTM[p0:p0 + run, slot, :], waits=vww, inc="avwd")
                        vww = []
                    vwv = E.val("avwd")
                    for h in range(8):
                        wt = []
                        if h == 0:
                            wt = [("awtv", wtv), ("avwd", vwv),
                                  ("atpv", E.val("atpv")), ("aofe", E.val("aofe"))]
                        E.op("tensor", lambda en, h=h, sw=sw, w=w:
                             en.matmul(PT[:, h * 128:h * 128 + w],
                                       VWIN[0:sw, h * 128:h * 128 + 128],
                                       WTS[0:sw, h * 128:h * 128 + w],
                                       start=True, stop=True),
                             waits=wt, inc=("aofd" if h == 7 else None))
                    ofd = E.val("aofd")
                    for h in range(8):
                        wt = [("aofd", ofd)] if h == 0 else []
                        E.op("vector", lambda en, h=h, w=w:
                             en.tensor_copy(OF[:, h * 128:h * 128 + w],
                                            PT[:, h * 128:h * 128 + w]),
                             waits=wt, inc=("aofe" if h == 7 else None))
                    ofe = E.val("aofe")
                    obase = len(osched)
                    osched.extend([(dc, half) for dc in range(8) for half in range(2)])
                    for dc in range(8):
                        for half in range(2):
                            k = obase + dc * 2 + half
                            ojit(k)
                            wt = [("aofe", ofe), ("awol", 16 * (k + 1))]
                            if dc == 0 and half == 0:
                                wt.append(("azs", zsv))
                            E.op("tensor", lambda en, dc=dc, half=half, k=k, w=w:
                                 en.matmul(PGR[0:w, half * 512:half * 512 + 512],
                                           OF[:, dc * 128:dc * 128 + w],
                                           WOS[:, k % 2, 0:512],
                                           start=(dc == 0), stop=False),
                                 waits=wt, inc="awof")
                    for rep in range(2):
                        for half in range(2):
                            E.op("tensor", lambda en, half=half, w=w:
                                 en.matmul(PGR[0:w, half * 512:half * 512 + 512],
                                           A["TRIU"][0:1, 0:w],
                                           BOR[0:1, half * 512:half * 512 + 512],
                                           start=False, stop=False))
                    for half in range(2):
                        E.op("tensor", lambda en, half=half, w=w, v=v:
                             en.matmul(PGR[0:w, half * 512:half * 512 + 512],
                                       AEDG[0:1, v, 0:w],
                                       BOR[0:1, half * 512:half * 512 + 512],
                                       start=False, stop=True),
                             inc=("awod" if half == 1 else None))
                    wodv = E.val("awod")
                    E.op("vector", lambda en, w=w: en.tensor_copy(
                        PYOS[0:w, :], PGR[0:w, :]),
                        waits=[("awod", wodv)], inc="apye")
                    pyev = E.val("apye")
                    r = 0
                    accw = [("apye", pyev)]
                    while r < w:
                        tabs = tq + r
                        cs = tabs // 128
                        p0 = tabs % 128
                        run = min(w - r, 128 - p0)
                        E.dma("gpsimd", out=X[p0:p0 + run, s, cs, :],
                              in_=PYOS[r:r + run, :], waits=accw, inc="aacc",
                              accum_op=OP.add)
                        nacc[0] += 1
                        accw = []
                        r += run
        E.op("gpsimd", lambda en: en.nop(), waits=[("aacc", 16 * nacc[0])])
        E.flush()



WNAMES = ["L0_attn_a_w_qkv", "L0_attn_a_w_out", "L0_attn_a_b_out",
          "L0_attn_b_Wq", "L0_attn_b_Wk", "L0_attn_b_Wv", "L0_attn_b_Wo",
          "L0_attn_b_W_gamma", "L0_attn_b_b_gamma", "L0_attn_b_W_lambda",
          "L0_attn_b_b_lambda", "L0_ffn_w1", "L0_ffn_b1", "L0_ffn_w2", "L0_ffn_b2"]
WSHAPES = {"L0_attn_a_w_qkv": [1024, 3072], "L0_attn_a_w_out": [1024, 1024],
           "L0_attn_a_b_out": [1024], "L0_attn_b_Wq": [1024, 1024],
           "L0_attn_b_Wk": [1024, 1024], "L0_attn_b_Wv": [1024, 1024],
           "L0_attn_b_Wo": [1024, 1024], "L0_attn_b_W_gamma": [2, 512],
           "L0_attn_b_b_gamma": [2, 512], "L0_attn_b_W_lambda": [2, 512],
           "L0_attn_b_b_lambda": [2, 512], "L0_ffn_w1": [4096, 1024],
           "L0_ffn_b1": [4096], "L0_ffn_w2": [1024, 4096], "L0_ffn_b2": [1024]}
CSHAPES = {"C_ident": [128, 128], "C_triu": [128, 128],
           "CA_mask": [3, 128, QW], "CA_sel": [3, 128, 64],
           "CA_selT": [3, 64, 128], "CA_m2": [3, 64, QW], "CA_edge": [3, 1, 128]}


def build_nc(phases=("attn_a", "ffn1", "attn_b", "ffn2")):
    nc = bass.Bass("TRN2", target_bir_lowering=False)
    A = {}
    A["x"] = nc.dram_tensor("x", [SPC, T, D], F32, kind="ExternalInput").ap()
    for nm in WNAMES:
        A[nm] = nc.dram_tensor(nm, WSHAPES[nm], F32, kind="ExternalInput").ap()
    for nm in CSHAPES:
        A[nm] = nc.dram_tensor(nm, CSHAPES[nm], F32, kind="ExternalInput").ap()
    A["y"] = nc.dram_tensor("y", [SPC, T, D], F32, kind="ExternalOutput").ap()
    for nm, shp in [("w1t", [1024, 4096]), ("w2t", [4096, 1024]),
                    ("wqt", [1024, 1024]), ("wkt", [1024, 1024]),
                    ("wvt", [1024, 1024]), ("wot", [1024, 1024])]:
        A[nm] = nc.dram_tensor(nm, shp, F32, kind="Internal").ap()
    E = Emit(nc)
    with (
        nc.sbuf_tensor("X", [128, SPC, NCH, D], F32) as Xs,
        nc.sbuf_tensor("IDENT", [128, 128], F32) as IDENT,
        nc.sbuf_tensor("TRIU", [128, 128], F32) as TRIU,
        nc.sbuf_tensor("B1S", [128, 32], F32) as B1S,
        nc.psum_tensor("PALL", [128, 4096], F32) as PALL,
        nc.Block() as block,
    ):
        A.update(X=Xs[:], IDENT=IDENT, TRIU=TRIU, B1S=B1S, PALL=PALL[:])
        E.block = block
        with (
            nc.sbuf_tensor("WSLAB", [128, 2, 2048], F32) as WSLAB,
            nc.sbuf_tensor("TSLAB", [128, 2, 16, 128], F32) as TSLAB,
            nc.sbuf_tensor("SGSLAB", [128, 2, 2048], F32) as SGSLAB,
            nc.sbuf_tensor("ALCOL", [128, 4], F32) as ALCOL,
            nc.sbuf_tensor("ALW2H", [128, 16], F32) as ALW2H,
            nc.sbuf_tensor("ALW2", [128, 8], F32) as ALW2,
        ):
            A.update(WSLAB=WSLAB, TSLAB=TSLAB, SGSLAB=SGSLAB, ALCOL=ALCOL,
                     ALW2H=ALW2H, ALW2=ALW2)
            emit_prologue(E, A)
        E.barrier()
        for ph in phases:
            if ph == "attn_a":
                emit_attn_a(E, A)
            elif ph == "attn_b":
                emit_attn_b(E, A)
            elif ph in ("ffn1", "ffn2"):
                emit_ffn(E, A, "f")
            E.barrier()
        yr = A["y"].rearrange("s (c p) d -> p s c d", p=128)
        E.dma("gpsimd", out=yr, in_=A["X"], inc="ystore")
        E.op("gpsimd", lambda en: en.nop(), waits=[("ystore", E.val("ystore"))])
        E.flush()
    return nc


def kernel(**inputs):
    consts = host_consts()
    nc = build_nc()
    in_maps = []
    xfull = np.ascontiguousarray(np.asarray(inputs["L0_attn_a_x"], dtype=np.float32))
    for core in range(NCORES):
        m = {"x": xfull[core * SPC:(core + 1) * SPC]}
        for nm in WNAMES:
            m[nm] = np.ascontiguousarray(np.asarray(inputs[nm], dtype=np.float32))
        m.update(consts)
        in_maps.append(m)
    res = run_bass_kernel_spmd(nc, in_maps, core_ids=list(range(NCORES)))
    return np.concatenate([r["y"] for r in res.results], axis=0)


# version 02 (final)
# speedup vs baseline: 1.0159x; 1.0159x over previous
import numpy as np
from contextlib import ExitStack
import concourse.bass as bass
import concourse.mybir as mybir
from concourse.bass_utils import run_bass_kernel_spmd
from concourse.alu_op_type import AluOpType as OP

F32 = mybir.dt.float32
AF = mybir.ActivationFunctionType

B, T, D = 16, 2048, 1024
NCORES = 8
SPC = 2
NCH = T // 128
HID = 4096
QW = 122
NEG = -1.0e30
ENGS = ("tensor", "vector", "scalar", "sync", "gpsimd")
AX = mybir.AxisListType.X


class Emit:
    CHAINED = ("vector", "scalar")

    def __init__(self, nc):
        self.nc = nc
        self.block = None
        self.buf = {e: [] for e in ENGS}
        self.sems = {}
        self.cnt = {}
        self.nbar = 0
        self.last_evt = {e: None for e in ENGS}

    def sem(self, name):
        if name not in self.sems:
            self.sems[name] = self.nc.alloc_semaphore(name)
            self.cnt[name] = 0
        return self.sems[name]

    def val(self, name):
        self.sem(name)
        return self.cnt[name]

    def op(self, eng, build, waits=(), inc=None, inc_amt=1):
        waits = list(waits)
        if eng in self.CHAINED:
            if self.last_evt[eng] is not None:
                waits.append(self.last_evt[eng])
            if inc is None:
                inc = eng[0] + "self"
                inc_amt = 1
        waits = tuple((s, v) for (s, v) in waits if v > 0)
        for (s, v) in waits:
            assert s in self.cnt and v <= self.cnt[s], (s, v, self.cnt.get(s))
        h = None
        if inc is not None:
            h = self.sem(inc)
            self.cnt[inc] += inc_amt
            if eng in self.CHAINED:
                self.last_evt[eng] = (inc, self.cnt[inc])

        def f(en, build=build, waits=waits, h=h, amt=inc_amt):
            for (s, v) in waits:
                en.wait_ge(self.sems[s], v)
            ins = build(en)
            if h is not None:
                ins.then_inc(h, amt)
        self.buf[eng].append(f)

    def dma(self, eng, out, in_, waits=(), inc=None, **kw):
        if inc is None and eng in self.CHAINED:
            inc = eng[0] + "dma"
        self.op(eng, lambda en: en.dma_start(out=out, in_=in_, **kw),
                waits=waits, inc=inc, inc_amt=16)

    def flush(self):
        for e in ENGS:
            ops = self.buf[e]
            self.buf[e] = []
            if ops:
                getattr(self.block, e)(lambda en, ops=ops: [f(en) for f in ops])

    def barrier(self):
        self.nbar += 1
        tgt = self.nbar * len(ENGS)
        bar = self.sem("bar")
        for e in ENGS:
            w = self.last_evt[e]
            def f(en, tgt=tgt, w=w):
                if w is not None and w[1] > 0:
                    en.wait_ge(self.sems[w[0]], w[1])
                en.nop().then_inc(bar, 1)
                en.wait_ge(bar, tgt)
            self.buf[e].append(f)
        self.cnt["bar"] = tgt
        self.flush()



def build_attn_a_consts():
    KS = 4
    NW_T = (T - KS) // 2 + 1
    wins = [(2 * n, 2 * n + KS) for n in range(NW_T)]
    subs = []
    j = 0
    while j * QW < T:
        tq = j * QW
        w = min(QW, T - tq)
        s0 = max(tq - 3, 0)
        s1 = min(tq + w + 3, T)
        n0 = max(0, -(-(tq - 3) // 2))
        n1 = min(NW_T - 1, (tq + w - 1) // 2)
        subs.append((tq, w, s0, s1 - s0, n0, n1 - n0 + 1))
        j += 1

    def tables(tq, w, s0, sw, n0, nw):
        mask = np.full((128, QW), NEG, np.float32)
        sel = np.zeros((128, 64), np.float32)
        selT = np.zeros((64, 128), np.float32)
        m2 = np.zeros((64, QW), np.float32)
        for nl in range(nw):
            lo, hi = wins[n0 + nl]
            for s in range(lo, hi):
                sl = s - s0
                if 0 <= sl < sw:
                    sel[sl, nl] = 1.0
                    selT[nl, sl] = 1.0
            for t in range(lo, hi):
                tl = t - tq
                if 0 <= tl < w:
                    m2[nl, tl] = 1.0
                    for s in range(lo, hi):
                        sl = s - s0
                        if 0 <= sl < sw:
                            mask[sl, tl] = 0.0
        return mask, sel, selT, m2

    var_of, packs = [], {}
    edge = {}
    cnt = np.zeros(T)
    for (lo, hi) in wins:
        cnt[lo:hi] += 1
    for (i, st) in enumerate(subs):
        v = 0 if i == 0 else (2 if i == len(subs) - 1 else 1)
        var_of.append(v)
        tb = tables(*st)
        tq, w = st[0], st[1]
        er = np.zeros((1, 128), np.float32)
        for tl in range(w):
            if cnt[tq + tl] == 1:
                er[0, tl] = -1.0
        if v not in packs:
            packs[v] = tb
            edge[v] = er
        else:
            for a, b in zip(packs[v], tb):
                assert np.array_equal(a, b)
            assert np.array_equal(edge[v], er)
    return dict(CM=np.stack([packs[v][0] for v in range(3)]),
                CS=np.stack([packs[v][1] for v in range(3)]),
                CST=np.stack([packs[v][2] for v in range(3)]),
                CM2=np.stack([packs[v][3] for v in range(3)]),
                CE=np.stack([edge[v] for v in range(3)]),
                subs=subs, var_of=var_of)


ACONSTS = build_attn_a_consts()


def host_consts():
    return {
        "C_ident": np.eye(128, dtype=np.float32),
        "C_triu": np.triu(np.ones((128, 128), np.float32)),
        "CA_mask": ACONSTS["CM"], "CA_sel": ACONSTS["CS"],
        "CA_selT": ACONSTS["CST"], "CA_m2": ACONSTS["CM2"],
        "CA_edge": ACONSTS["CE"],
    }


def emit_stats(E, xap, SQ, AG):
    E.op("vector", lambda en: en.tensor_tensor(out=SQ, in0=xap, in1=xap, op=OP.mult))
    E.op("vector", lambda en: en.reduce_sum(AG[:, 0:1], xap, axis=AX))
    E.op("vector", lambda en: en.reduce_sum(AG[:, 1:2], SQ, axis=AX))
    E.op("vector", lambda en: en.tensor_scalar(
        out=AG[:, 0:1], in0=AG[:, 0:1], scalar1=1.0 / 1024.0, scalar2=None, op0=OP.mult))
    E.op("vector", lambda en: en.tensor_tensor(
        out=AG[:, 3:4], in0=AG[:, 0:1], in1=AG[:, 0:1], op=OP.mult))
    E.op("vector", lambda en: en.tensor_scalar(
        out=AG[:, 1:2], in0=AG[:, 1:2], scalar1=1.0 / 1024.0, scalar2=None, op0=OP.mult))
    E.op("vector", lambda en: en.tensor_tensor(
        out=AG[:, 1:2], in0=AG[:, 1:2], in1=AG[:, 3:4], op=OP.subtract))


def ln_chunk(E, p, xap, ST, AG, XN, inc=None, waits=(), vwaits=()):
    if waits:
        E.op("vector", lambda en: en.nop(), waits=waits)
    if vwaits:
        E.op("vector", lambda en: en.nop(), waits=vwaits)
    emit_stats(E, xap, XN, AG)
    E.op("vector", lambda en: en.nop(), inc=p + "lnv")
    v = E.val(p + "lnv")
    E.op("scalar", lambda en: en.activation(AG[:, 2:3], AG[:, 1:2], AF.Ln, bias=np.float32(1e-5)),
         waits=[(p + "lnv", v)])
    E.op("scalar", lambda en: en.activation(AG[:, 2:3], AG[:, 2:3], AF.Exp, scale=np.float32(-0.5)),
         inc=p + "lns")
    s = E.val(p + "lns")
    E.op("vector", lambda en: en.tensor_scalar(
        out=XN, in0=xap, scalar1=AG[:, 0:1], scalar2=AG[:, 2:3],
        op0=OP.subtract, op1=OP.mult), waits=[(p + "lns", s)], inc=inc)



def emit_prologue(E, A):
    xr = A["x"].rearrange("s (c p) d -> p s c d", p=128)
    E.dma("sync", out=A["X"][:], in_=xr, inc="xload")
    E.dma("sync", out=A["IDENT"][:], in_=A["C_ident"][:, :], inc="cload")
    E.dma("sync", out=A["TRIU"][:], in_=A["C_triu"][:, :], inc="cload")
    E.dma("sync", out=A["B1S"][:], in_=A["L0_ffn_b1"].rearrange("(c p) -> p c", p=128),
          inc="cload", allow_slow_non_contiguous=True)
    cl = E.val("cload")

    WS = A["WSLAB"]; TO = A["TSLAB"]; SG = A["SGSLAB"]; AL = A["ALCOL"]
    ALW2 = A["ALW2"]

    for j in range(16):
        sl, hh = j // 2, j % 2
        wt = [("pwal2", E.val("pwal2") - 1)] if j >= 2 else []
        sem = "pwld%d" % (j % 2)
        E.dma("sync", out=WS[:, j % 2, :],
              in_=A["L0_ffn_w2"][sl * 128:(sl + 1) * 128, hh * 2048:(hh + 1) * 2048],
              waits=wt, inc=sem)
        lv = E.val(sem)
        E.op("vector", lambda en, j=j: en.reduce_sum(
            A["ALW2H"][:, j:j + 1], WS[:, j % 2, :], axis=AX,
            apply_absolute_value=True), waits=[(sem, lv)], inc="pwal2")
    for sl in range(8):
        E.op("vector", lambda en, sl=sl: en.tensor_tensor(
            out=ALW2[:, sl:sl + 1], in0=A["ALW2H"][:, 2 * sl:2 * sl + 1],
            in1=A["ALW2H"][:, 2 * sl + 1:2 * sl + 2], op=OP.add))

    jobs = []
    for sl in range(32):
        jobs.append((A["L0_ffn_w1"][sl * 128:(sl + 1) * 128, :],
                     A["w1t"][:, sl * 128:(sl + 1) * 128].rearrange("(a p) h -> p a h", p=128),
                     8, "solo", None, 1.0 / 1024))
    for sl in range(8):
        for hh in range(2):
            jobs.append((A["L0_ffn_w2"][sl * 128:(sl + 1) * 128, hh * 2048:(hh + 1) * 2048],
                         A["w2t"][hh * 2048:(hh + 1) * 2048, sl * 128:(sl + 1) * 128]
                         .rearrange("(a p) h -> p a h", p=128),
                         16, "col", ALW2[:, sl:sl + 1], 1.0 / 4096))
    for nm in ("Wq", "Wk", "Wv", "Wo"):
        for sl in range(8):
            jobs.append((A["L0_attn_b_" + nm][sl * 128:(sl + 1) * 128, :],
                         A[nm.lower() + "t"][:, sl * 128:(sl + 1) * 128]
                         .rearrange("(a p) h -> p a h", p=128),
                         8, "none", None, None))

    for (ji, (src, dst, nt, mode, a_ap, a_sc)) in enumerate(jobs):
        slot = ji % 2
        width = nt * 128
        ldsem = "pwld%d" % slot
        stsem = "pwst%d" % slot
        wt = []
        if ji == 0:
            wt.append(("pwal2", E.val("pwal2") - 1))
        elif ji == 1:
            wt.append(("pwal2", E.val("pwal2")))
        else:
            wt.append((stsem, E.val(stsem)))
        E.dma("sync", out=WS[:, slot, 0:width], in_=src, waits=wt, inc=ldsem)
        lv = E.val(ldsem)
        if mode != "none":
            if mode == "solo":
                E.op("vector", lambda en, slot=slot, width=width: en.reduce_sum(
                    AL[:, 0:1], WS[:, slot, 0:width], axis=AX,
                    apply_absolute_value=True), waits=[(ldsem, lv)], inc="pwal")
                aw = [("pwal", E.val("pwal"))]
                a_use = AL[:, 0:1]
            else:
                aw = []
                a_use = a_ap
            E.op("scalar", lambda en, slot=slot, width=width:
                 en.sign(SG[:, slot, 0:width], WS[:, slot, 0:width]),
                 waits=[(ldsem, lv)], inc="pwsg")
            sv = E.val("pwsg")
            E.op("vector", lambda en, slot=slot, width=width, a_use=a_use, a_sc=a_sc:
                 en.tensor_scalar(out=WS[:, slot, 0:width], in0=SG[:, slot, 0:width],
                                  scalar1=a_use, scalar2=a_sc,
                                  op0=OP.mult, op1=OP.mult),
                 waits=[("pwsg", sv)] + aw, inc="pwbin")
            bw = ("pwbin", E.val("pwbin"))
        else:
            bw = (ldsem, lv)
        stv = E.val(stsem)
        for i in range(nt):
            reg = A["PALL"][:, 512 + (i % 2) * 512: 512 + (i % 2) * 512 + 128]
            wts = [bw, ("cload", cl)]
            if E.val("pwtv") >= 2:
                wts.append(("pwtv", E.val("pwtv") - 1))
            E.op("tensor", lambda en, reg=reg, slot=slot, i=i:
                 en.transpose(reg, WS[:, slot, i * 128:(i + 1) * 128], A["IDENT"][:]),
                 waits=wts, inc="pwtt")
            tv = E.val("pwtt")
            ew = [("pwtt", tv)] + ([(stsem, stv)] if i == 0 else [])
            E.op("vector", lambda en, reg=reg, i=i, slot=slot:
                 en.tensor_copy(TO[:, slot, i, :], reg),
                 waits=ew, inc="pwtv")
        ev = E.val("pwtv")
        E.dma("gpsimd", out=dst, in_=TO[:, slot, 0:nt, :],
              waits=[("pwtv", ev)], inc=stsem)
    E.op("gpsimd", lambda en: en.nop(),
         waits=[("pwst0", E.val("pwst0")), ("pwst1", E.val("pwst1"))])
    E.op("sync", lambda en: en.nop(), waits=[("xload", E.val("xload"))])
    E.flush()



def emit_ffn(E, A, p):
    nc = E.nc
    X = A["X"]
    E.ffn_inst = getattr(E, "ffn_inst", 0) + 1
    tp = p + str(E.ffn_inst)
    with (
        nc.sbuf_tensor(tp + "XN", [128, 2, 1024], F32) as XN,
        nc.sbuf_tensor(tp + "XNF", [128, 8, 256], F32) as XNF,
        nc.sbuf_tensor(tp + "HS", [128, 2, 256], F32) as HS,
        nc.sbuf_tensor(tp + "W1S", [128, 2, 8, 256], F32) as W1S,
        nc.sbuf_tensor(tp + "W2S", [128, 2, 2, 1024], F32) as W2S,
        nc.sbuf_tensor(tp + "ST", [128, 2, 6], F32) as ST,
        nc.sbuf_tensor(tp + "AG", [128, 4], F32) as AG,
        nc.sbuf_tensor(tp + "MR", [128, 32], F32) as MR,
        nc.sbuf_tensor(tp + "RS", [128, 32], F32) as RS,
        nc.sbuf_tensor(tp + "B2R", [1, 1024], F32) as B2R,
    ):
        E.dma("sync", out=B2R[0:1, :], in_=A["L0_ffn_b2"].rearrange("(a d) -> a d", a=1),
              inc=p + "b2l")
        b2lv = E.val(p + "b2l")
        PALL = A["PALL"]
        PY = [PALL[:, 0:1024], PALL[:, 1024:2048]]
        PH = [PALL[:, 2048:2304], PALL[:, 2560:2816]]
        PT = PALL[:, 3072:4096]

        for gc in range(32):
            s, c = gc // 16, gc % 16
            xap = X[:, s, c, :]
            emit_stats(E, xap, XN[:, 0, :], AG)
            E.op("vector", lambda en, gc=gc: en.tensor_copy(MR[:, gc:gc + 1], AG[:, 0:1]))
            E.op("vector", lambda en, gc=gc: en.tensor_copy(RS[:, gc:gc + 1], AG[:, 1:2]),
                 inc=p + "st")
        sv = E.val(p + "st")
        E.op("scalar", lambda en: en.activation(RS[:], RS[:], AF.Ln, bias=np.float32(1e-5)),
             waits=[(p + "st", sv)])
        E.op("scalar", lambda en: en.activation(RS[:], RS[:], AF.Exp, scale=np.float32(-0.5)),
             inc=p + "rstd")
        rv = E.val(p + "rstd")

        m10 = E.val(p + "mm1")
        h0 = E.val(p + "hsf")
        l1s = [E.val(p + "w1l0"), E.val(p + "w1l1")]
        l2s = [E.val(p + "w2l0"), E.val(p + "w2l1")]

        def load_w1(k):
            wt = [(p + "mm1", m10 + 2 * (k - 1))] if k >= 2 else []
            E.dma("sync", out=W1S[:, k % 2, :, :],
                  in_=A["w1t"][:, (k % 16) * 256:(k % 16) * 256 + 256]
                  .rearrange("(a p) h -> p a h", p=128),
                  waits=wt, inc=p + "w1l%d" % (k % 2))

        def load_w2(k):
            wt = [(p + "hsf", h0 + 2 * (k - 1))] if k >= 2 else []
            E.dma("sync", out=W2S[:, k % 2, :, :],
                  in_=A["w2t"][(k % 16) * 256:(k % 16) * 256 + 256, :]
                  .rearrange("(a p) d -> p a d", p=128),
                  waits=wt, inc=p + "w2l%d" % (k % 2))

        load_w1(0); load_w2(0); load_w1(1); load_w2(1)

        for gi, g in enumerate(range(16)):
            s, c0 = (2 * g) // 16, (2 * g) % 16
            for lc in range(2):
                gc = 2 * g + lc
                xap = X[:, s, c0 + lc, :]
                wt = [(p + "rstd", rv)]
                if gi > 0:
                    wt.append((p + "pyfree", E.val(p + "pyfree")))
                E.op("vector", lambda en, xap=xap, lc=lc, gc=gc: en.tensor_scalar(
                    out=XN[:, lc, :], in0=xap, scalar1=MR[:, gc:gc + 1],
                    scalar2=RS[:, gc:gc + 1], op0=OP.subtract, op1=OP.mult),
                    waits=wt, inc=p + "ln")
            lnv = E.val(p + "ln")
            for i in range(16):
                lc, dc = i // 8, i % 8
                reg = PT[:, (i % 2) * 512:(i % 2) * 512 + 128]
                wt = [(p + "ln", lnv)]
                if E.val(p + "tpv") >= 1:
                    wt.append((p + "tpv", E.val(p + "tpv") - 1))
                E.op("tensor", lambda en, reg=reg, lc=lc, dc=dc:
                     en.transpose(reg, XN[:, lc, dc * 128:(dc + 1) * 128], A["IDENT"][:]),
                     waits=wt, inc=p + "tpt")
                tv = E.val(p + "tpt")
                E.op("vector", lambda en, reg=reg, lc=lc, dc=dc:
                     en.tensor_copy(XNF[:, dc, lc * 128:(lc + 1) * 128], reg),
                     waits=[(p + "tpt", tv)], inc=p + "tpv")
            xnfv = E.val(p + "tpv")

            for hc in range(32):
                k = gi * 16 + hc // 2
                if hc % 2 == 0:
                    if 2 <= k + 1 < 256:
                        load_w1(k + 1)
                    if 2 <= k + 1 < 256:
                        load_w2(k + 1)
                wt = [(p + "tpv", xnfv),
                      (p + "w1l%d" % (k % 2), l1s[k % 2] + 16 * (k // 2 + 1))]
                if hc >= 2:
                    wt.append((p + "act", E.val(p + "act") - 1))
                elif gi > 0:
                    wt.append((p + "act", E.val(p + "act")))
                for dc in range(8):
                    E.op("tensor", lambda en, hc=hc, dc=dc, k=k:
                         en.matmul(PH[hc % 2][:, 0:256],
                                   W1S[:, k % 2, dc, (hc % 2) * 128:(hc % 2) * 128 + 128],
                                   XNF[:, dc, :],
                                   start=(dc == 0), stop=(dc == 7)),
                         waits=(wt if dc == 0 else ()),
                         inc=(p + "mm1" if dc == 7 else None))
                m1 = E.val(p + "mm1")
                wt = [(p + "mm1", m1)]
                if hc >= 2:
                    wt.append((p + "hsf", E.val(p + "hsf") - 1))
                E.op("scalar", lambda en, hc=hc: en.activation(
                    HS[:, hc % 2, :], PH[hc % 2][:, 0:256], AF.Gelu,
                    bias=A["B1S"][:, hc:hc + 1]),
                    waits=wt, inc=p + "act")
                av = E.val(p + "act")
                wt2 = [(p + "act", av),
                       (p + "w2l%d" % (k % 2), l2s[k % 2] + 16 * (k // 2 + 1))]
                if gi > 0 and hc == 0:
                    wt2.append((p + "pyfree", E.val(p + "pyfree")))
                for tc in range(2):
                    for half in range(2):
                        E.op("tensor", lambda en, hc=hc, tc=tc, half=half, k=k:
                             en.matmul(PY[tc][:, half * 512:half * 512 + 512],
                                       HS[:, hc % 2, tc * 128:tc * 128 + 128],
                                       W2S[:, k % 2, hc % 2, half * 512:half * 512 + 512],
                                       start=(hc == 0), stop=False),
                             waits=(wt2 if (tc == 0 and half == 0) else ()),
                             inc=(p + "hsf" if (tc == 1 and half == 1) else None))
            for tc in range(2):
                for half in range(2):
                    E.op("tensor", lambda en, tc=tc, half=half:
                         en.matmul(PY[tc][:, half * 512:half * 512 + 512],
                                   A["TRIU"][0:1, 0:128],
                                   B2R[0:1, half * 512:half * 512 + 512],
                                   start=False, stop=True),
                         waits=([(p + "b2l", b2lv)] if (tc == 0 and half == 0) else ()),
                         inc=(p + "mm2d" if (tc == 1 and half == 1) else None))
            m2 = E.val(p + "mm2d")
            for tc in range(2):
                E.op("vector", lambda en, tc=tc, s=s, c0=c0: en.tensor_tensor(
                    out=X[:, s, c0 + tc, :], in0=X[:, s, c0 + tc, :],
                    in1=PY[tc][:, :], op=OP.add),
                    waits=[(p + "mm2d", m2)],
                    inc=(p + "pyfree" if tc == 1 else None))
        E.flush()



def emit_attn_b(E, A):
    nc = E.nc
    X = A["X"]
    PALL = A["PALL"]
    PA = PALL[:, 0:1024]
    PB = PALL[:, 1024:2048]
    PC = PALL[:, 2048:3072]
    PD = PALL[:, 3072:4096]
    with ExitStack() as stk:
        XNYF = stk.enter_context(nc.sbuf_tensor("bXNYF", [128, 1024], F32))
        XNFPIV = stk.enter_context(nc.sbuf_tensor("bXNFPIV", [128, 1024], F32))
        AG = stk.enter_context(nc.sbuf_tensor("bAG", [128, 4], F32))
        QF = stk.enter_context(nc.sbuf_tensor("bQF", [128, 1024], F32))
        KT = stk.enter_context(nc.sbuf_tensor("bKT", [128, 1024], F32))
        SS = stk.enter_context(nc.sbuf_tensor("bSS", [128, 4], F32))
        MTS = stk.enter_context(nc.sbuf_tensor("bMTS", [128, 256], F32))
        RN = stk.enter_context(nc.sbuf_tensor("bRN", [128, 2], F32))
        GSLSB = stk.enter_context(nc.sbuf_tensor("bGSLSB", [128, 1024], F32))
        GAMKH = stk.enter_context(nc.sbuf_tensor("bGAMKH", [128, 1024], F32))
        LAMPF = stk.enter_context(nc.sbuf_tensor("bLAMPF", [128, 1024], F32))
        RHOPCB = stk.enter_context(nc.sbuf_tensor("bRHOPCB", [128, 1024], F32))
        LGKTL = stk.enter_context(nc.sbuf_tensor("bLGKTL", [128, 1024], F32))

        S0 = stk.enter_context(nc.sbuf_tensor("bS0", [128, 2048], F32))
        S1 = stk.enter_context(nc.sbuf_tensor("bS1", [128, 2048], F32))
        WBS = stk.enter_context(nc.sbuf_tensor("bWBS", [128, 2, 1024], F32))
        GPB = stk.enter_context(nc.sbuf_tensor("bGPB", [128, 4096], F32))
        XN = XNYF; YF = XNYF; VT = XNYF
        PCR = GAMKH[0:1, 0:1024]
        VF = XNFPIV
        XNF = XNFPIV
        PIV = XNFPIV
        GS = GSLSB; LSB = GSLSB
        GAM = GAMKH; KH = GAMKH
        LAM = LAMPF; PF = LAMPF
        RHO = RHOPCB; PCB = RHOPCB
        LG = LGKTL; KTL = LGKTL
        SH = [S0, S1]

        pr = [("L0_attn_b_W_gamma", 0), ("L0_attn_b_b_gamma", 1),
              ("L0_attn_b_W_lambda", 2), ("L0_attn_b_b_lambda", 3)]
        for (nm, row) in pr:
            wt = [("bgpt", E.val("bgpt"))] if row > 0 else []
            for h in range(2):
                E.dma("sync", out=PCR[0:1, h * 512:(h + 1) * 512],
                      in_=A[nm][h:h + 1, :], waits=wt, inc="bprl")
                wt = []
            plv = E.val("bprl")
            for half in range(2):
                wt = [("bgpv", E.val("bgpv"))]
                if half == 0:
                    wt.append(("bprl", plv))
                E.op("tensor", lambda en, half=half:
                     en.matmul(PD[:, 0:512], A["TRIU"][0:1, 0:128],
                               PCR[0:1, half * 512:half * 512 + 512],
                               start=True, stop=True),
                     waits=wt, inc="bgpt")
                tv = E.val("bgpt")
                E.op("vector", lambda en, row=row, half=half:
                     en.tensor_copy(GPB[:, row * 1024 + half * 512:row * 1024 + half * 512 + 512],
                                    PD[:, 0:512]),
                     waits=[("bgpt", tv)], inc="bgpv")
        gpv = E.val("bgpv")

        nld = [0]
        sched = []

        def jit(g):
            while nld[0] <= min(g + 1, len(sched) - 1):
                k = nld[0]
                which, dc = sched[k]
                wt = []
                if k >= 2:
                    wt.append(("bwf", min(k - 1, E.val("bwf"))))
                E.dma("sync", out=WBS[:, k % 2, :],
                      in_=A[which][dc * 128:(dc + 1) * 128, :], waits=wt,
                      inc="bwl%d" % (k % 2))
                nld[0] += 1

        for s in range(SPC):
            w0 = [("bgpv", gpv)]
            if s > 0:
                w0.append(("byprd", E.val("byprd")))
            E.op("vector", lambda en: en.memset(S0[:], 0.0), waits=w0)
            E.op("vector", lambda en: en.memset(S1[:], 0.0), inc="bsup")
            for c in range(NCH):
                it = s * NCH + c
                base = it * 32
                sched.extend([("wqt", dc) for dc in range(8)])
                sched.extend([("wkt", dc) for dc in range(8)])
                sched.extend([("wvt", dc) for dc in range(8)])
                sched.extend([("wot", dc) for dc in range(8)])
                xap = X[:, s, c, :]
                lnw = [("bwf", it * 32)] if it > 0 else []
                ln_chunk(E, "b", xap, None, AG, XN[:], inc="bln", vwaits=lnw)
                lnv = E.val("bln")
                sup = E.val("bsup")
                for i in range(8):
                    reg = PD[:, (i % 2) * 512:(i % 2) * 512 + 128]
                    wt = [("bln", lnv)]
                    if i < 2:
                        wt.append(("bsup", sup))
                    else:
                        wt.append(("btev", E.val("btev") - 1))
                    E.op("tensor", lambda en, reg=reg, i=i:
                         en.transpose(reg, XN[:, i * 128:(i + 1) * 128], A["IDENT"][:]),
                         waits=wt, inc="btet")
                    tv = E.val("btet")
                    ew = [("btet", tv)]
                    if i == 0 and it > 0:
                        ew.append(("bktl", E.val("bktl")))
                    E.op("vector", lambda en, reg=reg, i=i:
                         en.tensor_copy(XNF[:, i * 128:(i + 1) * 128], reg),
                         waits=ew, inc="btev")
                xnfv = E.val("btev")
                for dc in range(8):
                    jit(base + dc)
                    ks = base + dc
                    wq = [("btev", xnfv), ("bwl%d" % (ks % 2), 16 * (ks // 2 + 1))]
                    if it > 0 and dc == 0:
                        wq.append(("byfv", E.val("byfv")))
                    for dkc in range(8):
                        E.op("tensor", lambda en, dc=dc, dkc=dkc, sl=(base + dc):
                             en.matmul(PA[:, dkc * 128:dkc * 128 + 128],
                                       WBS[:, sl % 2, dkc * 128:dkc * 128 + 128],
                                       XNF[:, dc * 128:dc * 128 + 128],
                                       start=(dc == 0 and dkc % 4 == 0),
                                       stop=(dc == 7 and dkc % 4 == 3)),
                             waits=(wq if dkc == 0 else ()),
                             inc=("bwf" if dkc == 7 else None))
                E.op("vector", lambda en: en.tensor_copy(QF[:, 0:512], PA[:, 0:512]),
                     waits=[("bwf", base + 8)])
                E.op("vector", lambda en: en.tensor_copy(QF[:, 512:1024], PA[:, 512:1024]),
                     inc="bqev")
                qev = E.val("bqev")
                for dc in range(8):
                    jit(base + 8 + dc)
                    ks = base + 8 + dc
                    wk = [("bwl%d" % (ks % 2), 16 * (ks // 2 + 1))]
                    if it > 0 and dc == 0:
                        wk.append(("bres", E.val("bres")))
                    for half in range(2):
                        E.op("tensor", lambda en, dc=dc, half=half, sl=(base + 8 + dc):
                             en.matmul(PB[:, half * 512:half * 512 + 512],
                                       XNF[:, dc * 128:dc * 128 + 128],
                                       WBS[:, sl % 2, half * 512:half * 512 + 512],
                                       start=(dc == 0), stop=(dc == 7)),
                             waits=(wk if half == 0 else ()),
                             inc=("bwf" if half == 1 else None))
                E.op("vector", lambda en: en.tensor_copy(KT[:, 0:512], PB[:, 0:512]),
                     waits=[("bwf", base + 16)])
                E.op("vector", lambda en: en.tensor_copy(KT[:, 512:1024], PB[:, 512:1024]),
                     inc="bkev")
                E.op("vector", lambda en: en.tensor_tensor(
                    out=GS[:], in0=KT[:], in1=KT[:], op=OP.mult),
                    waits=[("btes", E.val("btes"))] if it > 0 else [])
                for h in range(2):
                    E.op("vector", lambda en, h=h: en.reduce_sum(
                        SS[:, h:h + 1], GS[:, h * 512:h * 512 + 512], axis=AX))
                E.op("vector", lambda en: en.tensor_scalar(
                    out=SS[:, 2:4], in0=SS[:, 0:2], scalar1=1e-20, scalar2=None,
                    op0=OP.max), inc="bssv")
                ssv = E.val("bssv")
                E.op("scalar", lambda en: en.activation(RN[:], SS[:, 2:4], AF.Ln),
                     waits=[("bssv", ssv)])
                E.op("scalar", lambda en: en.activation(RN[:], RN[:], AF.Exp, scale=np.float32(-0.5)),
                     inc="brnv")
                rnv = E.val("brnv")
                for h in range(2):
                    E.op("vector", lambda en, h=h: en.tensor_scalar(
                        out=KT[:, h * 512:h * 512 + 512], in0=KT[:, h * 512:h * 512 + 512],
                        scalar1=RN[:, h:h + 1], scalar2=None, op0=OP.mult),
                        waits=([("brnv", rnv)] if h == 0 else ()),
                        inc=("bkhat" if h == 1 else None))
                khv = E.val("bkhat")
                for dc in range(8):
                    jit(base + 16 + dc)
                    ks = base + 16 + dc
                    wv = [("bwl%d" % (ks % 2), 16 * (ks // 2 + 1))]
                    if it > 0 and dc == 0:
                        wv.append(("bpcbv", E.val("bpcbv")))
                        wv.append(("bmtsv", E.val("bmtsv")))
                    for half in range(2):
                        E.op("tensor", lambda en, dc=dc, half=half, sl=(base + 16 + dc):
                             en.matmul(PC[:, half * 512:half * 512 + 512],
                                       XNF[:, dc * 128:dc * 128 + 128],
                                       WBS[:, sl % 2, half * 512:half * 512 + 512],
                                       start=(dc == 0), stop=(dc == 7)),
                             waits=(wv if half == 0 else ()),
                             inc=("bwf" if half == 1 else None))
                E.op("vector", lambda en: en.tensor_copy(VT[:, 0:512], PC[:, 0:512]),
                     waits=[("bwf", base + 24)])
                E.op("vector", lambda en: en.tensor_copy(VT[:, 512:1024], PC[:, 512:1024]),
                     inc="bvev")
                vev = E.val("bvev")
                for i in range(8):
                    reg = PD[:, (i % 2) * 512:(i % 2) * 512 + 128]
                    wt = [("bvev", vev), ("btev", E.val("btev") - 1)]
                    E.op("tensor", lambda en, reg=reg, i=i:
                         en.transpose(reg, VT[:, i * 128:(i + 1) * 128], A["IDENT"][:]),
                         waits=wt, inc="btet")
                    tv = E.val("btet")
                    ew = [("btet", tv)]
                    if i == 0:
                        ew.append(("bktl", E.val("bktl")))
                    E.op("vector", lambda en, reg=reg, i=i:
                         en.tensor_copy(VF[:, i * 128:(i + 1) * 128], reg),
                         waits=ew, inc="btev")
                vfv = E.val("btev")
                for h in range(2):
                    for vc in range(4):
                        j = h * 4 + vc
                        E.op("tensor", lambda en, h=h, j=j:
                             en.matmul(PC[:, h * 128:h * 128 + 128],
                                       VF[:, j * 128:j * 128 + 128],
                                       QF[:, j * 128:j * 128 + 128],
                                       start=(j % 4 == 0), stop=(j % 4 == 3)),
                             waits=([("btev", vfv), ("bqev", qev),
                                     ("bpcbv", E.val("bpcbv"))] if j == 0 else ()),
                             inc=("bgrd" if j == 7 else None))
                grd = E.val("bgrd")
                for h in range(2):
                    E.op("vector", lambda en, h=h: en.tensor_tensor(
                        out=MTS[:, h * 128:h * 128 + 128], in0=PC[:, h * 128:h * 128 + 128],
                        in1=A["TRIU"][:], op=OP.mult),
                        waits=([("bgrd", grd)] if h == 0 else ()),
                        inc=("bmtsv" if h == 1 else None))
                mtsv = E.val("bmtsv")
                E.op("vector", lambda en: en.tensor_tensor(
                    out=GS[:], in0=KT[:], in1=GPB[:, 0:1024], op=OP.mult),
                    waits=[("bkhat", khv)])
                E.op("vector", lambda en: en.tensor_tensor(
                    out=GS[:], in0=GS[:], in1=GPB[:, 1024:2048], op=OP.add),
                    inc="bzg")
                zg = E.val("bzg")
                gw = [("bzg", zg)]
                if it > 0:
                    gw.append(("bsud", E.val("bsud")))
                E.op("scalar", lambda en: en.activation(GAM[:], GS[:], AF.Exp, scale=np.float32(-1.0)),
                     waits=gw, inc="bze")
                ze = E.val("bze")
                E.op("vector", lambda en: en.tensor_scalar(
                    out=GAM[:], in0=GAM[:], scalar1=1.0, scalar2=None, op0=OP.add),
                    waits=[("bze", ze)])
                E.op("vector", lambda en: en.reciprocal(GAM[:], GAM[:]))
                E.op("vector", lambda en: en.tensor_tensor(
                    out=GS[:], in0=KT[:], in1=GPB[:, 2048:3072], op=OP.mult),
                    waits=[("bze", ze)])
                E.op("vector", lambda en: en.tensor_tensor(
                    out=GS[:], in0=GS[:], in1=GPB[:, 3072:4096], op=OP.add),
                    inc="bzl")
                zl = E.val("bzl")
                lw = [("bzl", zl)]
                if it > 0:
                    lw.append(("byfv", E.val("byfv")))
                E.op("scalar", lambda en: en.activation(LAM[:], GS[:], AF.Exp, scale=np.float32(-1.0)),
                     waits=lw, inc="bzle")
                zle = E.val("bzle")
                E.op("vector", lambda en: en.tensor_scalar(
                    out=LAM[:], in0=LAM[:], scalar1=1.0, scalar2=None, op0=OP.add),
                    waits=[("bzle", zle)])
                E.op("vector", lambda en: en.reciprocal(LAM[:], LAM[:]))
                E.op("vector", lambda en: en.tensor_scalar(
                    out=RHO[:], in0=LAM[:], scalar1=-1.0, scalar2=1.0,
                    op0=OP.mult, op1=OP.add))
                E.op("vector", lambda en: en.tensor_tensor(
                    out=RHO[:], in0=RHO[:], in1=GAM[:], op=OP.mult), inc="brho")
                rhov = E.val("brho")
                lgw = [("brho", rhov)]
                if it > 0:
                    lgw.append(("byprd", E.val("byprd")))
                E.op("scalar", lambda en: en.activation(LG[:], RHO[:], AF.Ln),
                     waits=lgw, inc="blg")
                lgv = E.val("blg")
                for half in range(2):
                    E.op("tensor", lambda en, half=half:
                         en.matmul(PB[:, half * 512:half * 512 + 512],
                                   A["TRIU"][:], LG[:, half * 512:half * 512 + 512],
                                   start=True, stop=True),
                         waits=([("blg", lgv), ("bkev", E.val("bkev"))] if half == 0 else ()),
                         inc=("bLd" if half == 1 else None))
                ld = E.val("bLd")
                E.op("vector", lambda en: en.tensor_copy(LSB[:], PB[:]),
                     waits=[("bLd", ld)], inc="blsb")
                lsbv = E.val("blsb")
                E.op("scalar", lambda en: en.activation(PIV[:], PB[:], AF.Exp, scale=np.float32(-1.0)),
                     waits=[("bLd", ld), ("bgrd", grd), ("blsb", lsbv)])
                for half in range(2):
                    E.op("tensor", lambda en, half=half:
                         en.matmul(PD[0:1, half * 512:half * 512 + 512],
                                   A["IDENT"][:, 127:128],
                                   LSB[:, half * 512:half * 512 + 512],
                                   start=True, stop=True),
                         waits=([("blsb", lsbv), ("btev", E.val("btev"))]
                                if half == 0 else ()),
                         inc=("bpcl" if half == 1 else None))
                pclv = E.val("bpcl")
                E.op("scalar", lambda en: en.activation(
                    PCR[0:1, :], PD[0:1, 0:1024], AF.Exp),
                     waits=[("bpcl", pclv)], inc="bpcr")
                pcrv = E.val("bpcr")
                E.op("vector", lambda en: en.tensor_scalar(
                    out=KTL[:], in0=RHO[:], scalar1=-1.0, scalar2=1.0,
                    op0=OP.mult, op1=OP.add), waits=[("bLd", ld)])
                E.op("vector", lambda en: en.tensor_tensor(
                    out=KTL[:], in0=KTL[:], in1=KT[:], op=OP.mult))
                E.op("vector", lambda en: en.tensor_tensor(
                    out=KTL[:], in0=KTL[:], in1=PIV[:], op=OP.mult),
                    waits=[("bpcr", pcrv)], inc="bktl")
                ktlv = E.val("bktl")
                for half in range(2):
                    E.op("tensor", lambda en, half=half:
                         en.matmul(PC[:, half * 512:half * 512 + 512],
                                   A["TRIU"][0:1, 0:128], PCR[0:1, half * 512:half * 512 + 512],
                                   start=True, stop=True),
                         waits=([("bpcr", pcrv), ("bvev", vev), ("bmtsv", mtsv)]
                                if half == 0 else ()),
                         inc=("bpcd" if half == 1 else None))
                pcd = E.val("bpcd")
                E.op("vector", lambda en: en.tensor_copy(PCB[:], PC[:]),
                     waits=[("bpcd", pcd), ("bktl", ktlv)], inc="bpcbv")
                E.op("vector", lambda en: en.tensor_tensor(
                    out=KH[:], in0=KTL[:], in1=PCB[:], op=OP.mult), inc="bkh")
                khx = E.val("bkh")
                for i in range(8):
                    reg = PD[:, (i % 2) * 512:(i % 2) * 512 + 128]
                    if i >= 2:
                        wt = [("blsb", lsbv), ("btes", E.val("btes") - 1)]
                    else:
                        wt = [("blsb", lsbv), ("btev", E.val("btev")),
                              ("bpcr", pcrv)]
                    E.op("tensor", lambda en, reg=reg, i=i:
                         en.transpose(reg, LSB[:, i * 128:(i + 1) * 128], A["IDENT"][:]),
                         waits=wt, inc="btet")
                    tv = E.val("btet")
                    E.op("scalar", lambda en, reg=reg, i=i:
                         en.activation(PF[:, i * 128:(i + 1) * 128], reg, AF.Exp),
                         waits=[("btet", tv)], inc="btes")
                pfv = E.val("btes")
                for h in range(2):
                    for dkc in range(4):
                        blk = PA[:, (h * 4 + dkc) * 128:(h * 4 + dkc) * 128 + 128]
                        E.op("tensor", lambda en, h=h, dkc=dkc, blk=blk:
                             en.matmul(blk, KTL[:, h * 512 + dkc * 128:h * 512 + dkc * 128 + 128],
                                       MTS[:, h * 128:h * 128 + 128],
                                       start=True, stop=False),
                             waits=([("bmtsv", mtsv), ("bktl", ktlv), ("bqev", qev),
                                     ("bsup", E.val("bsup"))] if (h == 0 and dkc == 0) else ()))
                        for vc in range(4):
                            E.op("tensor", lambda en, h=h, dkc=dkc, vc=vc, blk=blk:
                                 en.matmul(blk, SH[h][:, vc * 512 + dkc * 128:vc * 512 + dkc * 128 + 128],
                                           QF[:, (h * 4 + vc) * 128:(h * 4 + vc) * 128 + 128],
                                           start=False, stop=(vc == 3)),
                                 inc=("byprd" if (h == 1 and dkc == 3 and vc == 3) else None))
                yprd = E.val("byprd")
                for h in range(2):
                    for vc in range(4):
                        i = h * 4 + vc
                        reg = PD[:, (i % 2) * 512:(i % 2) * 512 + 512]
                        wt = [("bkh", khx), ("btes", pfv)]
                        if i >= 2:
                            wt.append(("bsup", E.val("bsup") - 1))
                        E.op("tensor", lambda en, h=h, vc=vc, reg=reg:
                             en.matmul(reg, VT[:, h * 512 + vc * 128:h * 512 + vc * 128 + 128],
                                       KH[:, h * 512:h * 512 + 512],
                                       start=True, stop=True),
                             waits=wt, inc="bsud")
                        sud = E.val("bsud")
                        E.op("vector", lambda en, h=h, vc=vc: en.tensor_tensor(
                            out=SH[h][:, vc * 512:vc * 512 + 512],
                            in0=SH[h][:, vc * 512:vc * 512 + 512],
                            in1=PCB[:, h * 512:h * 512 + 512], op=OP.mult),
                            waits=([("byprd", yprd)] if i == 0 else ()))
                        E.op("vector", lambda en, h=h, vc=vc, reg=reg: en.tensor_tensor(
                            out=SH[h][:, vc * 512:vc * 512 + 512],
                            in0=SH[h][:, vc * 512:vc * 512 + 512],
                            in1=reg, op=OP.add),
                            waits=[("bsud", sud)], inc="bsup")
                sudall = E.val("bsud")
                E.op("vector", lambda en: en.tensor_tensor(
                    out=YF[:], in0=PA[:], in1=PF[:], op=OP.mult),
                    waits=[("byprd", yprd), ("btes", pfv), ("bsud", sudall)], inc="byfv")
                yfv = E.val("byfv")
                for dyc in range(8):
                    jit(base + 24 + dyc)
                    ks = base + 24 + dyc
                    ww = [("bwl%d" % (ks % 2), 16 * (ks // 2 + 1)), ("byfv", yfv)]
                    if dyc == 0:
                        ww.append(("blsb", lsbv))
                        ww.append(("bpcr", pcrv))
                    for half in range(2):
                        E.op("tensor", lambda en, dyc=dyc, half=half, sl=(base + 24 + dyc):
                             en.matmul(PB[:, half * 512:half * 512 + 512],
                                       YF[:, dyc * 128:dyc * 128 + 128],
                                       WBS[:, sl % 2, half * 512:half * 512 + 512],
                                       start=(dyc == 0), stop=(dyc == 7)),
                             waits=(ww if half == 0 else ()),
                             inc=("bwf" if half == 1 else None))
                E.op("vector", lambda en, s=s, c=c: en.tensor_tensor(
                    out=X[:, s, c, :], in0=X[:, s, c, :], in1=PB[:], op=OP.add),
                    waits=[("bwf", base + 32)], inc="bres")
        E.flush()



def emit_attn_a(E, A):
    nc = E.nc
    X = A["X"]
    PALL = A["PALL"]
    PPR = PALL
    PGR = PALL[:, 0:1024]
    PS = PALL[:, 1024:2048]
    PH = PALL[:, 2048:3072]
    PT = PALL[:, 3072:4096]
    subs, var_of = ACONSTS["subs"], ACONSTS["var_of"]
    with ExitStack() as stk:
        KF = stk.enter_context(nc.sbuf_tensor("aKF", [128, 8, 384], F32))
        QF = stk.enter_context(nc.sbuf_tensor("aQF", [128, 8, 384], F32))
        VTM = stk.enter_context(nc.sbuf_tensor("aVTM", [128, 3, 1024], F32))
        OF = stk.enter_context(nc.sbuf_tensor("aOF", [128, 1024], F32))
        XNA = stk.enter_context(nc.sbuf_tensor("aXNA", [128, 1024], F32))
        XNFA = stk.enter_context(nc.sbuf_tensor("aXNFA", [128, 1024], F32))
        ZSB = stk.enter_context(nc.sbuf_tensor("aZSB", [128, 1024], F32))
        SPS = stk.enter_context(nc.sbuf_tensor("aSPS", [64, 1024], F32))
        WQT = stk.enter_context(nc.sbuf_tensor("aWQT", [128, 2, 1024], F32))
        WOS = stk.enter_context(nc.sbuf_tensor("aWOS", [128, 2, 512], F32))
        CMK = stk.enter_context(nc.sbuf_tensor("aCMK", [128, 3, QW], F32))
        CSL = stk.enter_context(nc.sbuf_tensor("aCSL", [128, 3, 64], F32))
        CSLT = stk.enter_context(nc.sbuf_tensor("aCSLT", [64, 3, 128], F32))
        CM2 = stk.enter_context(nc.sbuf_tensor("aCM2", [64, 3, QW], F32))
        AEDG = stk.enter_context(nc.sbuf_tensor("aAEDG", [1, 3, 128], F32))
        BOB2 = stk.enter_context(nc.sbuf_tensor("aBOB2", [1, 1024], F32))
        AG = stk.enter_context(nc.sbuf_tensor("aAG", [128, 4], F32))
        WTS = XNA
        ZP = XNFA
        VWIN = XNFA
        PYOS = ZSB
        BOR = BOB2[0:1, :]
        ST = None
        E.dma("sync", out=CMK[:], in_=A["CA_mask"].rearrange("v s t -> s v t"), inc="acl")
        E.dma("sync", out=CSL[:], in_=A["CA_sel"].rearrange("v s n -> s v n"), inc="acl")
        E.dma("sync", out=CSLT[:], in_=A["CA_selT"].rearrange("v n s -> n v s"), inc="acl")
        E.dma("sync", out=CM2[:], in_=A["CA_m2"].rearrange("v n t -> n v t"), inc="acl")
        E.dma("sync", out=AEDG[:], in_=A["CA_edge"].rearrange("v a t -> a v t"), inc="acl")
        E.dma("sync", out=BOR, in_=A["L0_attn_a_b_out"].rearrange("(a d) -> a d", a=1),
              inc="acl")
        aclv = E.val("acl")
        bo2v = 0

        nql = [0]
        qsched = []

        def qjit(k_need):
            while nql[0] <= min(k_need + 1, len(qsched) - 1):
                k = nql[0]
                dc, third = qsched[k]
                wt = []
                if k >= 2:
                    wt.append(("aqf", min(k - 1, E.val("aqf"))))
                E.dma("sync", out=WQT[:, k % 2, :],
                      in_=A["L0_attn_a_w_qkv"][dc * 128:(dc + 1) * 128,
                                               third * 1024:(third + 1) * 1024],
                      waits=wt, inc="aql%d" % (k % 2))
                nql[0] += 1

        nol = [0]
        osched = []

        def ojit(k_need):
            while nol[0] <= min(k_need + 1, len(osched) - 1):
                k = nol[0]
                dc, half = osched[k]
                wt = []
                if k >= 2:
                    wt.append(("awof", min(k - 1, E.val("awof"))))
                E.dma("sync", out=WOS[:, k % 2, :],
                      in_=A["L0_attn_a_w_out"][dc * 128:(dc + 1) * 128,
                                               half * 512:(half + 1) * 512],
                      waits=wt, inc="awol%d" % (k % 2))
                nol[0] += 1

        nacc = [0]

        for s in range(SPC):
            g_of = []
            for (j, (tq, w, s0, sw, n0, nw)) in enumerate(subs):
                gj = max(0, -(-((s0 + sw) - 384) // 256))
                assert 256 * gj <= s0 and (s0 + sw) <= 256 * gj + 384 and gj <= 7
                g_of.append(gj)
            for g in range(8):
                if g == 0:
                    slots = [0, 1, 2]
                elif g == 7:
                    slots = [1]
                else:
                    slots = [1, 2]
                if g > 0:
                    shw = [("agrd", E.val("agrd")), ("aofd", E.val("aofd"))]
                    E.op("vector", lambda en: en.tensor_copy(
                        KF[:, :, 0:128], KF[:, :, 256:384]), waits=shw)
                    E.op("vector", lambda en: en.tensor_copy(
                        QF[:, :, 0:128], QF[:, :, 256:384]))
                    E.op("vector", lambda en: en.tensor_copy(
                        VTM[:, 0, :], VTM[:, 2, :]),
                        waits=[("avwd", E.val("avwd"))], inc="ashf")
                shfv = E.val("ashf")
                for sl in slots:
                    c = 2 * g + sl
                    lnw = [("aofd", E.val("aofd")), ("acl", aclv)]
                    ln_chunk(E, "a", X[:, s, c, :], ST, AG, XNA[:], inc="aln",
                             vwaits=lnw)
                    lnv = E.val("aln")
                    for i in range(8):
                        reg = PT[:, (i % 2) * 512:(i % 2) * 512 + 128]
                        wt = [("aln", lnv)]
                        if E.val("atpv") >= 1:
                            wt.append(("atpv", E.val("atpv") - 1))
                        if i < 2:
                            wt.append(("aofe", E.val("aofe")))
                        E.op("tensor", lambda en, reg=reg, i=i:
                             en.transpose(reg, XNA[:, i * 128:(i + 1) * 128], A["IDENT"][:]),
                             waits=wt, inc="atpt")
                        tv = E.val("atpt")
                        ew = [("atpt", tv)]
                        if i == 0:
                            ew.append(("azex", E.val("azex")))
                            ew.append(("avwd", E.val("avwd")))
                        E.op("vector", lambda en, reg=reg, i=i:
                             en.tensor_copy(XNFA[:, i * 128:(i + 1) * 128], reg),
                             waits=ew, inc="atpv")
                    xnfv = E.val("atpv")
                    base = len(qsched)
                    qsched.extend([(dc, th) for dc in range(8) for th in range(3)])
                    for dc in range(8):
                        for th in range(3):
                            k = base + dc * 3 + th
                            qjit(k)
                            wt = [("atpv", xnfv),
                                  ("aql%d" % (k % 2), 16 * (k // 2 + 1))]
                            if dc == 0 and th == 0:
                                wt.append(("apye", E.val("apye")))
                                wt.append(("aspr", E.val("aspr")))
                                wt.append(("awtv", E.val("awtv")))
                            if th < 2:
                                for oc in range(8):
                                    blk = th * 8 + oc
                                    E.op("tensor", lambda en, dc=dc, oc=oc, blk=blk, k=k:
                                         en.matmul(PPR[:, blk * 128:blk * 128 + 128],
                                                   WQT[:, k % 2, oc * 128:oc * 128 + 128],
                                                   XNFA[:, dc * 128:dc * 128 + 128],
                                                   start=(dc == 0 and blk % 4 == 0),
                                                   stop=(dc == 7 and blk % 4 == 3)),
                                         waits=(wt if oc == 0 else ()),
                                         inc=("aqf" if oc == 7 else None))
                            else:
                                for half in range(2):
                                    E.op("tensor", lambda en, dc=dc, half=half, k=k:
                                         en.matmul(PPR[:, 2048 + half * 512:2048 + half * 512 + 512],
                                                   XNFA[:, dc * 128:dc * 128 + 128],
                                                   WQT[:, k % 2, half * 512:half * 512 + 512],
                                                   start=(dc == 0), stop=(dc == 7)),
                                         waits=(wt if half == 0 else ()),
                                         inc=("aqf" if half == 1 else None))
                    E.op("tensor", lambda en: en.nop(), waits=[("aqf", E.val("aqf"))],
                         inc="aprd")
                    pv = E.val("aprd")
                    for oc in range(8):
                        E.op("vector", lambda en, oc=oc, sl=sl: en.tensor_copy(
                            QF[:, oc, sl * 128:sl * 128 + 128],
                            PPR[:, oc * 128:oc * 128 + 128]),
                            waits=([("aprd", pv)] if oc == 0 else ()))
                    for oc in range(8):
                        E.op("vector", lambda en, oc=oc, sl=sl: en.tensor_copy(
                            KF[:, oc, sl * 128:sl * 128 + 128],
                            PPR[:, 1024 + oc * 128:1024 + oc * 128 + 128]))
                    for half in range(2):
                        E.op("vector", lambda en, half=half, sl=sl: en.tensor_copy(
                            VTM[:, sl, half * 512:half * 512 + 512],
                            PPR[:, 2048 + half * 512:2048 + half * 512 + 512]),
                            inc=("apev" if half == 1 else None))
                pev = E.val("apev")
                for (j, gj) in enumerate(g_of):
                    if gj != g:
                        continue
                    tq, w, s0, sw, n0, nw = subs[j]
                    v = var_of[j]
                    sl0 = s0 - 256 * g
                    tq0 = tq - 256 * g
                    for h in range(8):
                        wt = []
                        if h == 0:
                            wt = [("apev", pev), ("ashf", shfv),
                                  ("apye", E.val("apye"))]
                        E.op("tensor", lambda en, h=h, sl0=sl0, sw=sw, tq0=tq0, w=w:
                             en.matmul(PGR[0:sw, h * 128:h * 128 + w],
                                       KF[:, h, sl0:sl0 + sw], QF[:, h, tq0:tq0 + w],
                                       start=True, stop=True),
                             waits=wt, inc=("agrd" if h == 7 else None))
                    grd = E.val("agrd")
                    for h in range(8):
                        wt = [("agrd", grd)] if h == 0 else []
                        E.op("vector", lambda en, h=h, sw=sw, w=w, v=v:
                             en.scalar_tensor_tensor(
                                 out=ZP[0:sw, h * 128:h * 128 + w],
                                 in0=PGR[0:sw, h * 128:h * 128 + w],
                                 scalar=1.0 / 32.0,
                                 in1=CMK[0:sw, v, 0:w],
                                 op0=OP.mult, op1=OP.add),
                             waits=wt, inc=("azs" if h == 7 else None))
                    zsv = E.val("azs")
                    aw = [("azs", zsv), ("aacc", 16 * nacc[0])]
                    for h in range(8):
                        E.op("scalar", lambda en, h=h, sw=sw, w=w:
                             en.activation(ZSB[0:sw, h * 128:h * 128 + w],
                                           ZP[0:sw, h * 128:h * 128 + w], AF.Exp),
                             waits=(aw if h == 0 else ()),
                             inc=("azex" if h == 7 else None))
                    zev = E.val("azex")
                    for h in range(8):
                        wt = []
                        if h == 0:
                            wt = [("azex", zev), ("aspr", E.val("aspr"))]
                        E.op("tensor", lambda en, h=h, sw=sw, w=w, v=v, nw=nw:
                             en.matmul(PS[0:nw, h * 128:h * 128 + w],
                                       CSL[0:sw, v, 0:nw], ZSB[0:sw, h * 128:h * 128 + w],
                                       start=True, stop=True),
                             waits=wt, inc=("asd" if h == 7 else None))
                    sdv = E.val("asd")
                    for h in range(8):
                        wt = [("asd", sdv)] if h == 0 else []
                        E.op("vector", lambda en, h=h, nw=nw, w=w:
                             en.tensor_scalar(out=SPS[0:nw, h * 128:h * 128 + w],
                                              in0=PS[0:nw, h * 128:h * 128 + w],
                                              scalar1=1.0, scalar2=None, op0=OP.add),
                             waits=wt)
                        E.op("vector", lambda en, h=h, nw=nw, w=w, v=v:
                             en.tensor_tensor(out=SPS[0:nw, h * 128:h * 128 + w],
                                              in0=SPS[0:nw, h * 128:h * 128 + w],
                                              in1=CM2[0:nw, v, 0:w], op=OP.subtract))
                        E.op("vector", lambda en, h=h, nw=nw, w=w:
                             en.reciprocal(SPS[0:nw, h * 128:h * 128 + w],
                                           SPS[0:nw, h * 128:h * 128 + w]))
                        E.op("vector", lambda en, h=h, nw=nw, w=w, v=v:
                             en.tensor_tensor(out=SPS[0:nw, h * 128:h * 128 + w],
                                              in0=SPS[0:nw, h * 128:h * 128 + w],
                                              in1=CM2[0:nw, v, 0:w], op=OP.mult),
                             inc=("aspr" if h == 7 else None))
                    spv = E.val("aspr")
                    for h in range(8):
                        wt = []
                        if h == 0:
                            wt = [("aspr", spv), ("awtv", E.val("awtv"))]
                        E.op("tensor", lambda en, h=h, sw=sw, w=w, v=v, nw=nw:
                             en.matmul(PH[0:sw, h * 128:h * 128 + w],
                                       CSLT[0:nw, v, 0:sw], SPS[0:nw, h * 128:h * 128 + w],
                                       start=True, stop=True),
                             waits=wt, inc=("ahd" if h == 7 else None))
                    hdv = E.val("ahd")
                    for h in range(8):
                        wt = [("ahd", hdv)] if h == 0 else []
                        E.op("vector", lambda en, h=h, sw=sw, w=w:
                             en.tensor_tensor(out=WTS[0:sw, h * 128:h * 128 + w],
                                              in0=ZSB[0:sw, h * 128:h * 128 + w],
                                              in1=PH[0:sw, h * 128:h * 128 + w], op=OP.mult),
                             waits=wt, inc=("awtv" if h == 7 else None))
                    wtv = E.val("awtv")
                    pieces = []
                    r = 0
                    while r < sw:
                        sabs = s0 + r
                        cs = sabs // 128
                        p0 = sabs % 128
                        run = min(sw - r, 128 - p0)
                        pieces.append((cs - 2 * g, p0, r, run))
                        r += run
                    vww = [("azex", zev), ("apev", pev), ("ashf", shfv)]
                    for (slot, p0, r0, run) in pieces:
                        E.dma("gpsimd", out=VWIN[r0:r0 + run, :],
                              in_=VTM[p0:p0 + run, slot, :], waits=vww, inc="avwd")
                        vww = []
                    vwv = E.val("avwd")
                    for h in range(8):
                        wt = []
                        if h == 0:
                            wt = [("awtv", wtv), ("avwd", vwv),
                                  ("atpv", E.val("atpv")), ("aofe", E.val("aofe"))]
                        E.op("tensor", lambda en, h=h, sw=sw, w=w:
                             en.matmul(PT[:, h * 128:h * 128 + w],
                                       VWIN[0:sw, h * 128:h * 128 + 128],
                                       WTS[0:sw, h * 128:h * 128 + w],
                                       start=True, stop=True),
                             waits=wt, inc=("aofd" if h == 7 else None))
                    ofd = E.val("aofd")
                    for h in range(8):
                        wt = [("aofd", ofd)] if h == 0 else []
                        E.op("vector", lambda en, h=h, w=w:
                             en.tensor_copy(OF[:, h * 128:h * 128 + w],
                                            PT[:, h * 128:h * 128 + w]),
                             waits=wt, inc=("aofe" if h == 7 else None))
                    ofe = E.val("aofe")
                    obase = len(osched)
                    osched.extend([(dc, half) for dc in range(8) for half in range(2)])
                    for dc in range(8):
                        for half in range(2):
                            k = obase + dc * 2 + half
                            ojit(k)
                            wt = [("aofe", ofe),
                                  ("awol%d" % (k % 2), 16 * (k // 2 + 1))]
                            if dc == 0 and half == 0:
                                wt.append(("azs", zsv))
                            E.op("tensor", lambda en, dc=dc, half=half, k=k, w=w:
                                 en.matmul(PGR[0:w, half * 512:half * 512 + 512],
                                           OF[:, dc * 128:dc * 128 + w],
                                           WOS[:, k % 2, 0:512],
                                           start=(dc == 0), stop=False),
                                 waits=wt, inc="awof")
                    for rep in range(2):
                        for half in range(2):
                            E.op("tensor", lambda en, half=half, w=w:
                                 en.matmul(PGR[0:w, half * 512:half * 512 + 512],
                                           A["TRIU"][0:1, 0:w],
                                           BOR[0:1, half * 512:half * 512 + 512],
                                           start=False, stop=False))
                    for half in range(2):
                        E.op("tensor", lambda en, half=half, w=w, v=v:
                             en.matmul(PGR[0:w, half * 512:half * 512 + 512],
                                       AEDG[0:1, v, 0:w],
                                       BOR[0:1, half * 512:half * 512 + 512],
                                       start=False, stop=True),
                             inc=("awod" if half == 1 else None))
                    wodv = E.val("awod")
                    E.op("vector", lambda en, w=w: en.tensor_copy(
                        PYOS[0:w, :], PGR[0:w, :]),
                        waits=[("awod", wodv)], inc="apye")
                    pyev = E.val("apye")
                    r = 0
                    accw = [("apye", pyev)]
                    while r < w:
                        tabs = tq + r
                        cs = tabs // 128
                        p0 = tabs % 128
                        run = min(w - r, 128 - p0)
                        E.dma("gpsimd", out=X[p0:p0 + run, s, cs, :],
                              in_=PYOS[r:r + run, :], waits=accw, inc="aacc",
                              accum_op=OP.add)
                        nacc[0] += 1
                        accw = []
                        r += run
        E.op("gpsimd", lambda en: en.nop(), waits=[("aacc", 16 * nacc[0])])
        E.flush()



WNAMES = ["L0_attn_a_w_qkv", "L0_attn_a_w_out", "L0_attn_a_b_out",
          "L0_attn_b_Wq", "L0_attn_b_Wk", "L0_attn_b_Wv", "L0_attn_b_Wo",
          "L0_attn_b_W_gamma", "L0_attn_b_b_gamma", "L0_attn_b_W_lambda",
          "L0_attn_b_b_lambda", "L0_ffn_w1", "L0_ffn_b1", "L0_ffn_w2", "L0_ffn_b2"]
WSHAPES = {"L0_attn_a_w_qkv": [1024, 3072], "L0_attn_a_w_out": [1024, 1024],
           "L0_attn_a_b_out": [1024], "L0_attn_b_Wq": [1024, 1024],
           "L0_attn_b_Wk": [1024, 1024], "L0_attn_b_Wv": [1024, 1024],
           "L0_attn_b_Wo": [1024, 1024], "L0_attn_b_W_gamma": [2, 512],
           "L0_attn_b_b_gamma": [2, 512], "L0_attn_b_W_lambda": [2, 512],
           "L0_attn_b_b_lambda": [2, 512], "L0_ffn_w1": [4096, 1024],
           "L0_ffn_b1": [4096], "L0_ffn_w2": [1024, 4096], "L0_ffn_b2": [1024]}
CSHAPES = {"C_ident": [128, 128], "C_triu": [128, 128],
           "CA_mask": [3, 128, QW], "CA_sel": [3, 128, 64],
           "CA_selT": [3, 64, 128], "CA_m2": [3, 64, QW], "CA_edge": [3, 1, 128]}


def build_nc(phases=("attn_a", "ffn1", "attn_b", "ffn2")):
    nc = bass.Bass("TRN2", target_bir_lowering=False)
    A = {}
    A["x"] = nc.dram_tensor("x", [SPC, T, D], F32, kind="ExternalInput").ap()
    for nm in WNAMES:
        A[nm] = nc.dram_tensor(nm, WSHAPES[nm], F32, kind="ExternalInput").ap()
    for nm in CSHAPES:
        A[nm] = nc.dram_tensor(nm, CSHAPES[nm], F32, kind="ExternalInput").ap()
    A["y"] = nc.dram_tensor("y", [SPC, T, D], F32, kind="ExternalOutput").ap()
    for nm, shp in [("w1t", [1024, 4096]), ("w2t", [4096, 1024]),
                    ("wqt", [1024, 1024]), ("wkt", [1024, 1024]),
                    ("wvt", [1024, 1024]), ("wot", [1024, 1024])]:
        A[nm] = nc.dram_tensor(nm, shp, F32, kind="Internal").ap()
    E = Emit(nc)
    with (
        nc.sbuf_tensor("X", [128, SPC, NCH, D], F32) as Xs,
        nc.sbuf_tensor("IDENT", [128, 128], F32) as IDENT,
        nc.sbuf_tensor("TRIU", [128, 128], F32) as TRIU,
        nc.sbuf_tensor("B1S", [128, 32], F32) as B1S,
        nc.psum_tensor("PALL", [128, 4096], F32) as PALL,
        nc.Block() as block,
    ):
        A.update(X=Xs[:], IDENT=IDENT, TRIU=TRIU, B1S=B1S, PALL=PALL[:])
        E.block = block
        with (
            nc.sbuf_tensor("WSLAB", [128, 2, 2048], F32) as WSLAB,
            nc.sbuf_tensor("TSLAB", [128, 2, 16, 128], F32) as TSLAB,
            nc.sbuf_tensor("SGSLAB", [128, 2, 2048], F32) as SGSLAB,
            nc.sbuf_tensor("ALCOL", [128, 4], F32) as ALCOL,
            nc.sbuf_tensor("ALW2H", [128, 16], F32) as ALW2H,
            nc.sbuf_tensor("ALW2", [128, 8], F32) as ALW2,
        ):
            A.update(WSLAB=WSLAB, TSLAB=TSLAB, SGSLAB=SGSLAB, ALCOL=ALCOL,
                     ALW2H=ALW2H, ALW2=ALW2)
            emit_prologue(E, A)
        E.barrier()
        for ph in phases:
            if ph == "attn_a":
                emit_attn_a(E, A)
            elif ph == "attn_b":
                emit_attn_b(E, A)
            elif ph in ("ffn1", "ffn2"):
                emit_ffn(E, A, "f")
            E.barrier()
        yr = A["y"].rearrange("s (c p) d -> p s c d", p=128)
        E.dma("gpsimd", out=yr, in_=A["X"], inc="ystore")
        E.op("gpsimd", lambda en: en.nop(), waits=[("ystore", E.val("ystore"))])
        E.flush()
    return nc


def kernel(**inputs):
    consts = host_consts()
    nc = build_nc()
    in_maps = []
    xfull = np.ascontiguousarray(np.asarray(inputs["L0_attn_a_x"], dtype=np.float32))
    for core in range(NCORES):
        m = {"x": xfull[core * SPC:(core + 1) * SPC]}
        for nm in WNAMES:
            m[nm] = np.ascontiguousarray(np.asarray(inputs[nm], dtype=np.float32))
        m.update(consts)
        in_maps.append(m)
    res = run_bass_kernel_spmd(nc, in_maps, core_ids=list(range(NCORES)))
    return np.concatenate([r["y"] for r in res.results], axis=0)
